# version 02
# baseline (speedup 1.0000x reference)
import numpy as np
import ml_dtypes
import concourse.bass as bass
import concourse.mybir as mybir
from concourse.bass_utils import run_bass_kernel_spmd

F32 = mybir.dt.float32
BF16 = mybir.dt.bfloat16
I32 = mybir.dt.int32
AF = mybir.ActivationFunctionType
ALU = mybir.AluOpType
AX = mybir.AxisListType
AX = mybir.AxisListType

B, S, D = 4, 4096, 1024
FT = D // 128
TOKCH = 512
NT = S // TOKCH
L0FFN_W2 = True
L0FFN_HST = True
L0FFN_GELU = True
L0FFN_NH = 64
L3FFN_PH = 'full'
DBG_L3 = False
DBG_L2 = False
L2FFN_BF = True
L2FFN_PAIRS = True
L2FFN_NBLK = 7
L2FFN_UPONLY = False
L2FFN_NPAIR = 5
L2A_NH = 8
L2A_NCH = 32


class Prog:
    ENGS = ("tensor", "vector", "scalar", "gpsimd", "sync")

    def __init__(self, nc):
        self.nc = nc
        self.q = {e: [] for e in self.ENGS}
        self._sem_handles = {}
        self.cnt = {}
        self._ctxs = []

    @staticmethod
    def _canon(name):
        h, _, t = name.rpartition("_")
        return h if (h and t.isdigit()) else name

    def sem(self, name):
        name = self._canon(name)
        if name not in self._sem_handles:
            cm = self.nc.semaphore(name)
            self._sem_handles[name] = cm.__enter__()
            self._ctxs.append(cm)
            self.cnt[name] = 0
            self.amt_class = getattr(self, "amt_class", {})
        return name

    def c(self, name):
        return self.cnt.get(self._canon(name), 0)

    def emit(self, eng, fn, waits=(), inc=None, amt=1):
        if inc is not None:
            inc = self.sem(inc)
            prev = self.amt_class.setdefault(inc, amt)
            assert prev == amt, f"sem {inc} mixes inc amounts {prev}/{amt}"
            self.cnt[inc] += amt
        waits = [(self._canon(s), v) for (s, v) in waits if v > 0]
        for s, _ in waits:
            self.sem(s)
        self.q[eng].append((tuple(waits), fn, inc, amt))
        return self.cnt[inc] if inc is not None else None

    def dma(self, eng, out, in_, inc, waits=()):
        return self.emit(eng, lambda e: e.dma_start(out=out, in_=in_), waits, inc, 16)

    def mm(self, out, lhsT, rhs, start, stop, waits=(), inc=None, tile_position=None):
        def fn(t, out=out, lhsT=lhsT, rhs=rhs):
            return t.matmul(out, lhsT, rhs, start=start, stop=stop,
                            tile_position=tile_position)
        return self.emit("tensor", fn, waits, inc)

    def act(self, out, in_, func, bias=0.0, scale=1.0, accum_out=None,
            waits=(), inc=None):
        def fn(s, out=out, in_=in_):
            return s.activation(out, in_, func, bias=bias, scale=scale,
                                accum_out=accum_out)
        return self.emit("scalar", fn, waits, inc)

    def vop(self, fn, waits=(), inc=None):
        return self.emit("vector", fn, waits, inc)

    def finalize(self):
        sems = self._sem_handles
        with self.nc.Block() as block:
            def runner(eng_name):
                def run(obj):
                    for waits, fn, inc, amt in self.q[eng_name]:
                        for s, v in waits:
                            obj.wait_ge(sems[s], v)
                        ins = fn(obj)
                        if inc is not None:
                            assert ins is not None, f"op with inc={inc} returned None"
                            ins.then_inc(sems[inc], amt)
                return run
            block.tensor(runner("tensor"))
            block.vector(runner("vector"))
            block.scalar(runner("scalar"))
            block.gpsimd(runner("gpsimd"))
            block.sync(runner("sync"))



def _slabs(w):
    d_in, d_out = w.shape
    assert d_in % 128 == 0
    return np.ascontiguousarray(
        np.asarray(w, dtype=np.float32).reshape(d_in // 128, 128, d_out))


def _bias_cols(b):
    d = b.shape[0]
    assert d % 128 == 0
    return np.ascontiguousarray(np.asarray(b, np.float32).reshape(d // 128, 128).T)



class Builder:
    def __init__(self, nc):
        self.nc = nc
        self.p = Prog(nc)
        self.uid = 0
        self.input_names = []

        self.x_in = self.inp("x_res", [D, S], F32)
        self.x_a = nc.dram_tensor("x_a", [D, S], F32).ap()
        self.x_b = nc.dram_tensor("x_b", [D, S], F32).ap()
        self.x_out = nc.dram_tensor("x_out", [D, S], F32, kind="ExternalOutput").ap()
        self.x_cur = self.x_in
        self._parity = 0
        self.store_sem = None

        self.xhat = nc.alloc_sbuf_tensor("xhat", [128, FT, S], BF16)
        self.xf = nc.alloc_sbuf_tensor("xf", [128, FT, TOKCH], F32)
        self.lnsq = nc.alloc_sbuf_tensor("lnsq", [128, FT, TOKCH], F32)
        self.rows = nc.alloc_sbuf_tensor("rows", [1, 4, TOKCH], F32)
        self.bcast = nc.alloc_sbuf_tensor("bcast", [128, 2, TOKCH], F32)
        self.xrs = nc.alloc_sbuf_tensor("xrs", [128, FT, TOKCH], F32)
        self.abf = nc.alloc_sbuf_tensor("abf", [128, 34816], BF16)
        self.af32 = nc.alloc_sbuf_tensor("af32", [128, 3456], F32)
        self._abf_off = 0
        self._af32_off = 0
        self.arena_guard = []
        self.ones_f32 = nc.alloc_sbuf_tensor("ones_f32", [128, 1], F32)
        self.ones_bf = nc.alloc_sbuf_tensor("ones_bf", [128, 1], BF16)
        self.ones_row = nc.alloc_sbuf_tensor("ones_row", [1, 128], F32)
        self.ones_row_bf = nc.alloc_sbuf_tensor("ones_row_bf", [1, 128], BF16)
        self.ceps = nc.alloc_sbuf_tensor("ceps", [128, 1], F32)
        self.scrap = nc.alloc_sbuf_tensor("scrap", [1, 8], F32)
        self.allones = nc.alloc_sbuf_tensor("allones", [128, 128], F32)
        self.maskTg = nc.alloc_sbuf_tensor("maskTg", [128, 128], F32)
        self.pd = [nc.alloc_psum_tensor(f"pd{i}", [128, 1024], F32) for i in range(2)]
        self.pb = [self.pd[0][:, 0:512], self.pd[0][:, 512:1024],
                   self.pd[1][:, 0:512], self.pd[1][:, 512:1024]] + \
                  [nc.alloc_psum_tensor(f"pb{i}", [128, 512], F32)[:] for i in range(4, 7)]
        self.ptps = nc.alloc_psum_tensor("ptps", [128, 1024], BF16)
        self.id_bf = nc.alloc_sbuf_tensor("id_bf", [128, 128], BF16)

        p = self.p
        p.emit("gpsimd", lambda g: g.memset(self.ones_f32[:], 1.0), inc="init")
        p.emit("gpsimd", lambda g: g.memset(self.ones_bf[:], 1.0), inc="init")
        p.emit("gpsimd", lambda g: g.memset(self.ones_row[:], 1.0), inc="init")
        p.emit("gpsimd", lambda g: g.memset(self.ones_row_bf[:], 1.0), inc="init")
        p.emit("gpsimd", lambda g: g.memset(self.ceps[:], 1e-5), inc="init")
        p.emit("gpsimd", lambda g: g.memset(self.scrap[:], 0.0), inc="init")
        p.emit("gpsimd", lambda g: g.memset(self.allones[:], 1.0), inc="init")
        p.emit("gpsimd", lambda g: g.memset(self.maskTg[:], 0.0), inc="init")
        p.emit("gpsimd", lambda g: g.affine_select(
            out=self.maskTg[:], in_=self.maskTg[:], compare_op=ALU.is_ge,
            fill=-1e30, base=0, pattern=[[1, 128]], channel_multiplier=-1),
            waits=[("init", p.c("init"))], inc="init")
        p.emit("gpsimd", lambda g: g.memset(self.id_bf[:], 1.0), inc="init")
        p.emit("gpsimd", lambda g: g.affine_select(
            out=self.id_bf[:], in_=self.id_bf[:], compare_op=ALU.is_equal,
            fill=0.0, base=0, pattern=[[-1, 128]], channel_multiplier=1),
            waits=[("init", p.c("init"))], inc="init")
        self.initw = ("init", p.c("init"))

    def inp(self, name, shape, dtype=F32):
        ap = self.nc.dram_tensor(name, shape, dtype, kind="ExternalInput").ap()
        self.input_names.append(name)
        return ap

    def fresh(self, tag):
        self.uid += 1
        return f"{tag}_{self.uid}"

    def sb(self, tag, shape, dtype):
        return self.nc.alloc_sbuf_tensor(self.fresh(tag), shape, dtype)

    def flush_v(self, sem, waits=()):
        return self.p.vop(lambda v: v.tensor_scalar_add(
            self.scrap[0:1, 0:1], self.scrap[0:1, 0:1], 0.0),
            waits=waits, inc=sem)

    def arena_reset(self):
        self._abf_off = 0
        self._af32_off = 0
        self.arena_guard = [self.store_sem] if self.store_sem else []

    def arena_mark(self):
        return (self._abf_off, self._af32_off)

    def arena_to(self, mark):
        self._abf_off, self._af32_off = mark

    def ab(self, shape, dtype):
        assert shape[0] in (1, 8, 16, 32, 64, 128)
        import math
        n = int(np.prod(shape[1:]))
        if dtype == BF16:
            o = self._abf_off
            self._abf_off += n
            assert self._abf_off <= 34816, f"bf16 arena overflow {self._abf_off}"
            v = self.abf[0:shape[0], o:o + n]
        else:
            o = self._af32_off
            self._af32_off += n
            assert self._af32_off <= 3456, f"f32 arena overflow {self._af32_off}"
            v = self.af32[0:shape[0], o:o + n]
        if len(shape) == 2:
            return v
        dims = "abcdef"[: len(shape) - 1]
        pat = f"p ({' '.join(dims)}) -> p {' '.join(dims)}"
        kw = {d: shape[i + 1] for i, d in enumerate(dims)}
        return v.rearrange(pat, **kw)

    def x_target(self):
        t = self.x_a if self._parity == 0 else self.x_b
        self._parity ^= 1
        return t

    def fm(self, hbm, t0, tn):
        return hbm[:, t0:t0 + tn].rearrange("(ft p) t -> p ft t", p=128)

    def ln_waits(self):
        return [self.store_sem] if self.store_sem else []

    def load_w(self, name, kc, m):
        ap = self.inp(name, [kc, 128, m], F32)
        t = self.ab([128, kc, m], BF16)
        sem = self.fresh("wld")
        self.p.dma("gpsimd", t, ap.rearrange("kc p m -> p kc m"), inc=sem,
                   waits=self.arena_guard)
        return t, (sem, self.p.c(sem))

    def load_f32(self, name, shape):
        ap = self.inp(name, shape, F32)
        t = self.ab(list(shape), F32)
        sem = self.fresh("bld")
        self.p.dma("gpsimd", t, ap, inc=sem, waits=self.arena_guard)
        return t, (sem, self.p.c(sem))

    def layernorm_to_xhat(self, waits=()):
        p = self.p
        ld = self.fresh("ln_ld")
        sqd = self.fresh("ln_sq")
        pbfree = self.fresh("ln_pbf")
        rowd = self.fresh("ln_row")
        bcd = self.fresh("ln_bcd")
        bce = self.fresh("ln_bce")
        xh = self.fresh("ln_xh")
        xh0 = p.c(xh)
        pbf0 = p.c(pbfree)
        bce0 = p.c(bce)
        for tc in range(NT):
            t0 = tc * TOKCH
            xv = self.xf[:]
            w = [self.initw] + list(waits)
            if tc >= 1:
                w.append((xh, xh0 + tc))
            p.dma("sync", xv, self.fm(self.x_cur, t0, TOKCH), inc=ld, waits=w)
            ldv = p.c(ld)
            p.act(self.lnsq[:], xv, AF.Square, waits=[(ld, ldv), (xh, xh0 + tc)], inc=sqd)
            for ft in range(FT):
                p.mm(self.pb[0][0:1, :], self.ones_f32[:], xv[:, ft],
                     start=(ft == 0), stop=(ft == FT - 1),
                     waits=[(ld, ldv), (pbfree, pbf0 + tc)] if ft == 0 else [])
            for ft in range(FT):
                p.mm(self.pb[1][0:1, :], self.ones_f32[:], self.lnsq[:, ft],
                     start=(ft == 0), stop=(ft == FT - 1),
                     waits=[(sqd, p.c(sqd))] if ft == 0 else [],
                     inc=rowd if ft == FT - 1 else None)
            mmd_v = p.c(rowd)
            mr = self.rows[:, 0, :]
            vr = self.rows[:, 2, :]
            p.vop(lambda v, mr=mr: v.tensor_scalar_mul(
                mr, self.pb[0][0:1, :], 1.0 / D), waits=[(rowd, mmd_v)])
            p.vop(lambda v, vr=vr, mr=mr: v.scalar_tensor_tensor(
                vr, mr, -1.0, mr, ALU.mult, ALU.mult))
            p.vop(lambda v, vr=vr: v.scalar_tensor_tensor(
                vr, self.pb[1][0:1, :], 1.0 / D, vr, ALU.mult, ALU.add), inc=pbfree)
            p.act(vr, vr, AF.Sqrt, bias=self.ceps[0:1, 0:1],
                  waits=[(pbfree, p.c(pbfree))], inc=rowd)
            p.vop(lambda v, vr=vr: v.reciprocal(vr, vr),
                  waits=[(rowd, p.c(rowd))], inc=rowd)
            rowv = p.c(rowd)
            p.mm(self.pb[2][:, 0:TOKCH], self.ones_row[:], mr, start=True, stop=True,
                 waits=[(rowd, rowv), (bce, bce0 + 2 * tc)])
            p.mm(self.pb[3][:, 0:TOKCH], self.ones_row[:], vr, start=True, stop=True,
                 inc=bcd)
            bcv = p.c(bcd)
            p.emit("scalar", lambda s: s.copy(self.bcast[:, 0], self.pb[2][:, 0:TOKCH]),
                   waits=[(bcd, bcv), (xh, xh0 + tc)], inc=bce)
            p.emit("scalar", lambda s: s.copy(self.bcast[:, 1], self.pb[3][:, 0:TOKCH]),
                   inc=bce)
            bev = p.c(bce)
            for ft in range(FT):
                dst = self.xhat[:, ft, t0:t0 + TOKCH]
                tmp = self.lnsq[:, ft]
                p.vop(lambda v, tmp=tmp, xvf=xv[:, ft]: v.tensor_sub(
                    tmp, xvf, self.bcast[:, 0]),
                    waits=[(bce, bev)] if ft == 0 else [])
                p.vop(lambda v, dst=dst, tmp=tmp: v.tensor_mul(
                    dst, tmp, self.bcast[:, 1]),
                    inc=xh if ft == FT - 1 else None)
        return xh, p.c(xh)

    def _mha64(self, wqa, wka, wva, woa, bqb, bqw, bkb, bkw, bob, bow,
               bvrow, bvw, x_tgt, wguard=(), alibi=False, rez=None):
        nc, p = self.nc, self.p
        attn_hbm = nc.dram_tensor(self.fresh("attnh"), [16, 64, S], BF16).ap()

        xh_sem, xh_cnt = self.layernorm_to_xhat(
            waits=self.ln_waits() + list(wguard))
        xw = (xh_sem, xh_cnt)

        attn_hg = self.ab([64, 2, S], BF16)
        allones = self.allones
        amark = self.arena_mark()
        wqs = self.ab([128, 2, 8, 128], BF16)
        wks = self.ab([128, 2, 8, 128], BF16)
        wvs = self.ab([128, 2, 8, 128], BF16)
        qT = self.ab([128, S], BF16)
        kT = self.ab([128, S], BF16)
        vtok = self.ab([128, 32, 130], BF16)
        pt2 = self.ab([128, 2, 1024], BF16)
        p.emit("gpsimd", lambda g: g.memset(vtok[:, :, 64:65], 1.0),
               waits=self.arena_guard, inc="init")
        p.emit("gpsimd", lambda g: g.memset(vtok[:, :, 129:130], 1.0), inc="init")
        onesw = ("init", p.c("init"))

        wld = self.fresh("wld")
        pj = self.fresh("pj")
        mmq = self.fresh("mmq")
        sS = self.fresh("sS")
        sP = self.fresh("sP")
        sPV = self.fresh("sPV")
        sNm = self.fresh("sNm")
        sASt = self.fresh("sASt")

        alibi2 = None
        alw = None
        sAl = self.fresh("sAl")
        if alibi:
            poscol = self.ab([128, 32], F32)
            alibi2 = self.ab([128, 2, 32], F32)
            p.emit("gpsimd", lambda g: g.iota(
                poscol, [[128, 32]], base=-4095, channel_multiplier=1,
                allow_small_or_imprecise_dtypes=True), inc="init")
            posw = ("init", p.c("init"))

        wld_of_buf = {0: 0, 1: 0}
        s_read = {0: 0, 1: 0}
        p_read = {0: 0, 1: 0}
        pvfree = 0
        for hg in range(8):
            wb = hg % 2
            for wa, ws in ((wqa, wqs), (wka, wks), (wva, wvs)):
                p.dma("gpsimd", ws[:, wb],
                      wa[:, :, hg * 128:(hg + 1) * 128].rearrange("kc p m -> p kc m"),
                      inc=wld, waits=self.arena_guard + list(wguard)
                      + [(mmq, wld_of_buf[wb])])
            wldv = p.c(wld)
            evac = {0: 0, 1: 0}
            for wsl, dstT, bias_t, sc in ((wqs[:, wb], qT, bqb, 0.125),
                                          (wks[:, wb], kT, bkb, 1.0)):
                for tc in range(NT):
                    t0 = tc * TOKCH
                    bki = tc % 2
                    pbk = self.pb[bki]
                    for ki in range(8):
                        w = [xw, (wld, wldv), (pj, evac[bki]), bqw, bkw,
                             (sP, s_read[0])] if ki == 0 else []
                        p.mm(pbk, wsl[:, ki, :], self.xhat[:, ki, t0:t0 + TOKCH],
                             start=(ki == 0), stop=(ki == 7), waits=w,
                             inc=mmq if ki == 7 else None)
                    p.vop(lambda v, dst=dstT[:, t0:t0 + TOKCH], pbk=pbk,
                          bc=bias_t[:, hg:hg + 1], sc=sc: v.tensor_scalar(
                              dst, pbk, bc, sc, ALU.add, ALU.mult),
                          waits=[(mmq, p.c(mmq))], inc=pj)
                    evac[bki] = p.c(pj)
            for tt in range(32):
                t0 = tt * 128
                bki = tt % 2
                pbk = self.pb[bki][:, 0:128]
                for ki in range(8):
                    w = [xw, (wld, wldv), (pj, evac[bki])] if ki == 0 else []
                    p.mm(pbk, self.xhat[:, ki, t0:t0 + 128], wvs[:, wb, ki, :],
                         start=(ki == 0), stop=False, waits=w)
                p.mm(pbk, self.ones_row_bf[:],
                     bvrow[0:1, hg * 128:(hg + 1) * 128], start=False, stop=True,
                     waits=[self.initw, bvw], inc=mmq)
                mqv = p.c(mmq)
                p.vop(lambda v, tt=tt, pbk=pbk: v.tensor_copy(
                    vtok[:, tt, 0:64], pbk[:, 0:64]),
                    waits=[(mmq, mqv), onesw])
                p.vop(lambda v, tt=tt, pbk=pbk: v.tensor_copy(
                    vtok[:, tt, 65:129], pbk[:, 64:128]), inc=pj)
                evac[bki] = p.c(pj)
            pjv = p.c(pj)
            wld_of_buf[wb] = p.c(mmq)
            if alibi:
                p.vop(lambda v, hg=hg: v.tensor_scalar_mul(
                    alibi2[:, 0, :], poscol, float(0.5 ** (2 * hg))),
                    waits=[posw, (sP, p.c(sP))], inc=sAl)
                p.vop(lambda v, hg=hg: v.tensor_scalar_mul(
                    alibi2[:, 1, :], poscol, float(0.5 ** (2 * hg + 1))),
                    inc=sAl)
                alw = (sAl, p.c(sAl))
            for qc in range(NT):
                q0 = qc * TOKCH
                for kt in range(32):
                    k0 = kt * 128
                    sb2 = self.pd[kt % 2]
                    pv2 = pt2[:, kt % 2]
                    p.mm(sb2[:, 0:TOKCH], kT[0:64, k0:k0 + 128],
                         qT[0:64, q0:q0 + TOKCH], start=True, stop=True,
                         waits=[(pj, pjv), (sP, s_read[kt % 2])],
                         tile_position=(0, 0))
                    p.mm(sb2[:, TOKCH:2 * TOKCH], kT[64:128, k0:k0 + 128],
                         qT[64:128, q0:q0 + TOKCH], start=True, stop=True,
                         tile_position=(64, 0), inc=sS)
                    if alibi:
                        p.act(pv2[:, 0:TOKCH], sb2[:, 0:TOKCH], AF.Exp,
                              bias=alibi2[:, 0, kt:kt + 1],
                              waits=[(sS, p.c(sS)), (sPV, p_read[kt % 2]),
                                     alw], inc=sP)
                        p.act(pv2[:, TOKCH:2 * TOKCH], sb2[:, TOKCH:2 * TOKCH],
                              AF.Exp, bias=alibi2[:, 1, kt:kt + 1], inc=sP)
                    else:
                        p.act(pv2, sb2[:, 0:2 * TOKCH], AF.Exp,
                              waits=[(sS, p.c(sS)), (sPV, p_read[kt % 2])], inc=sP)
                    s_read[kt % 2] = p.c(sP)
                    w_pv = [(sP, p.c(sP))]
                    if kt == 0:
                        w_pv += [(sNm, pvfree), onesw]
                    p.mm(self.pb[4][0:65, :], vtok[:, kt, 0:65],
                         pv2[:, 0:TOKCH], start=(kt == 0), stop=(kt == 31),
                         waits=w_pv)
                    p.mm(self.pb[5][0:65, :], vtok[:, kt, 65:130],
                         pv2[:, TOKCH:2 * TOKCH], start=(kt == 0), stop=(kt == 31),
                         inc=sPV)
                    p_read[kt % 2] = p.c(sPV)
                sPVv = p.c(sPV)
                for h2 in range(2):
                    pbo = self.pb[4 + h2]
                    p.vop(lambda v, pbo=pbo: v.reciprocal(
                        self.bcast[64:65, 1, :], pbo[64:65, :]),
                        waits=[(sPV, sPVv), (sNm, p.c(sNm))], inc=sNm)
                    p.mm(self.pb[6], allones[64:65, 0:128],
                         self.bcast[64:65, 1, :], start=True, stop=True,
                         waits=[(sNm, p.c(sNm))], inc=sNm)
                    p.emit("scalar", lambda s: s.copy(
                        self.bcast[:, 0], self.pb[6]),
                        waits=[(sNm, p.c(sNm))], inc=sNm)
                    p.vop(lambda v, pbo=pbo, h2=h2, q0=q0: v.tensor_mul(
                        attn_hg[:, h2, q0:q0 + TOKCH], pbo[0:64, :],
                        self.bcast[0:64, 0]),
                        waits=[(sNm, p.c(sNm))], inc=sNm)
                self.flush_v(sNm)
                pvfree = p.c(sNm)
            p.dma("sync", attn_hbm[2 * hg:2 * hg + 2, :, :].rearrange("h p t -> p h t"),
                  attn_hg[:], inc=sASt, waits=[(sNm, p.c(sNm))])
        attw = (sASt, p.c(sASt))

        self.arena_to(amark)
        wout_sb = self.ab([64, 16, D], BF16)
        attn_qc = self.ab([64, 16, TOKCH], BF16)
        yt = self.ab([128, 2, TOKCH], F32)
        w2g = self.fresh("w2g")
        p.dma("gpsimd", wout_sb, woa.rearrange("h p m -> p h m"),
              inc=w2g, waits=[attw] + list(wguard))
        w2v = p.c(w2g)
        aql = self.fresh("aql")
        sOut = self.fresh("sOut")
        sEv = self.fresh("sEv")
        sSt = self.fresh("sSt")
        xld = self.fresh("xld")
        ev_of_yt = {0: 0, 1: 0}
        aq_read = 0
        for qc in range(NT):
            q0 = qc * TOKCH
            xbuf = self.xrs[:]
            p.dma("sync", xbuf, self.fm(self.x_cur, q0, TOKCH), inc=xld,
                  waits=[(sSt, p.c(sSt))])
            xldv = p.c(xld)
            p.dma("sync", attn_qc[:], attn_hbm[:, :, q0:q0 + TOKCH].rearrange(
                "h p t -> p h t"), inc=aql,
                waits=[attw, (sOut, aq_read)])
            aqv = p.c(aql)
            for mi in range(FT):
                bki = mi % 2
                pbY = self.pb[bki]
                for h in range(16):
                    w = [(aql, aqv), (w2g, w2v),
                         (sEv, ev_of_yt[bki])] if h == 0 else []
                    p.mm(pbY, wout_sb[:, h, mi * 128:(mi + 1) * 128],
                         attn_qc[:, h, :], start=(h == 0), stop=(h == 15),
                         waits=w, inc=sOut if h == 15 else None)
                p.vop(lambda v, pbY=pbY, mi=mi, bki=bki: v.tensor_scalar(
                    yt[:, bki], pbY, bob[:, mi:mi + 1], None, ALU.add),
                    waits=[(sOut, p.c(sOut)), (xld, xldv), bow], inc=sEv)
                if rez is not None:
                    p.vop(lambda v, mi=mi, bki=bki, qc=qc: v.scalar_tensor_tensor(
                        yt[:, bki], yt[:, bki], rez[0],
                        self.xhat[:, mi, qc * TOKCH:(qc + 1) * TOKCH],
                        ALU.mult, ALU.add),
                        waits=[rez[1]], inc=sEv)
                p.vop(lambda v, mi=mi, bki=bki, xbuf=xbuf: v.tensor_add(
                    xbuf[:, mi], xbuf[:, mi], yt[:, bki]),
                    waits=[(sEv, p.c(sEv))], inc=sEv)
                ev_of_yt[bki] = p.c(sEv)
            aq_read = p.c(sOut)
            p.dma("sync", self.fm(x_tgt, q0, TOKCH), xbuf,
                  inc=sSt, waits=[(sEv, p.c(sEv))])
        self.x_cur = x_tgt
        self.store_sem = (sSt, p.c(sSt))


    def block_l0_attn(self):
        nc, p = self.nc, self.p
        x_tgt = self.x_target()
        self.arena_reset()
        wqa = self.inp("l0_wq", [8, 128, D], F32)
        wka = self.inp("l0_wk", [8, 128, D], F32)
        wva = self.inp("l0_wv", [8, 128, D], F32)
        woa = self.inp("l0_wout", [16, 64, D], F32)
        bqb, bqw = self.load_f32("l0_bq", [128, 8])
        bkb, bkw = self.load_f32("l0_bk", [128, 8])
        bob, bow = self.load_f32("l0_bout", [128, 8])
        bvr_ap = self.inp("l0_bv_row", [1, D], F32)
        bvrow = self.ab([1, D], BF16)
        bvs = self.fresh("bld")
        p.dma("gpsimd", bvrow, bvr_ap, inc=bvs, waits=self.arena_guard)
        bvw = (bvs, p.c(bvs))
        self._mha64(wqa, wka, wva, woa, bqb, bqw, bkb, bkw, bob, bow,
                    bvrow, bvw, x_tgt)

    def block_l3_attn(self):
        nc, p = self.nc, self.p
        x_tgt = self.x_target()
        self.arena_reset()
        bqb, bqw = self.load_f32("l3_bq", [128, 8])
        bkb, bkw = self.load_f32("l3_bk", [128, 8])
        bob, bow = self.load_f32("l3_bo", [128, 8])
        rzc, rzw = self.load_f32("l3_rz", [128, 1])
        bvr_ap = self.inp("l3_bv_row", [1, D], F32)
        bvrow = self.ab([1, D], BF16)
        bvs = self.fresh("bld")
        p.dma("gpsimd", bvrow, bvr_ap, inc=bvs, waits=self.arena_guard)
        bvw = (bvs, p.c(bvs))

        amark = self.arena_mark()
        wf = {}
        sFold = self.fresh("sFold")
        fld = self.fresh("fld")
        sFmm = self.fresh("sFmm")
        ab_sb = self.ab([8, 2, 1024], BF16)
        ab_f32 = self.ab([8, 1024], F32)
        wch = self.ab([128, 2, 1024], BF16)
        sAB = self.fresh("sAB")
        evac_of = {0: 0, 1: 0}
        for pi, pname in enumerate(("q", "k", "v", "o")):
            wsrc = self.inp(f"l3_w{pname}", [8, 128, D], F32)
            asrc = self.inp(f"l3_a{pname}", [8, D], F32)
            bsrc = self.inp(f"l3_bt{pname}", [8, D], F32)
            if pname == "o":
                wdst = nc.dram_tensor(self.fresh("wfo"), [16, 64, D], BF16).ap()
            else:
                wdst = nc.dram_tensor(self.fresh("wf"), [8, 128, D], BF16).ap()
            wf[pname] = wdst
            p.dma("gpsimd", ab_f32, asrc, inc=fld, waits=self.arena_guard)
            p.dma("gpsimd", ab_sb[:, 1], bsrc, inc=fld)
            p.vop(lambda v: v.tensor_scalar_mul(ab_sb[:, 0], ab_f32, 0.125),
                  waits=[(fld, p.c(fld))], inc=sAB)
            abv = p.c(sAB)
            for ki in range(8):
                wb = ki % 2
                p.dma("gpsimd", wch[:, wb],
                      wsrc[ki:ki + 1].rearrange("kc p m -> p (kc m)"), inc=fld,
                      waits=[(sFold, evac_of[wb])])
                fv = p.c(fld)
                pbk = self.pd[wb]
                p.mm(pbk[:, 0:TOKCH], ab_sb[:, 0, ki * 128:(ki + 1) * 128],
                     ab_sb[:, 1, 0:512], start=True, stop=False,
                     waits=[(sAB, abv), (fld, fv), (sFold, evac_of[wb])])
                p.mm(pbk[:, TOKCH:1024], ab_sb[:, 0, ki * 128:(ki + 1) * 128],
                     ab_sb[:, 1, 512:1024], start=True, stop=False)
                p.mm(pbk[:, 0:TOKCH], self.id_bf[:], wch[:, wb, 0:512],
                     start=False, stop=False, waits=[self.initw])
                p.mm(pbk[:, TOKCH:1024], self.id_bf[:], wch[:, wb, 512:1024],
                     start=False, stop=True, inc=sFmm)
                p.vop(lambda v, wb=wb, pbk=pbk: v.tensor_copy(
                    wch[:, wb], pbk[:, 0:1024]),
                    waits=[(sFmm, p.c(sFmm))], inc=sAB)
                if pname == "o":
                    p.dma("sync", wdst[2 * ki:2 * ki + 2].rearrange(
                        "h p m -> (h p) m"), wch[:, wb], inc=sFold,
                        waits=[(sAB, p.c(sAB))])
                else:
                    p.dma("sync", wdst[ki], wch[:, wb], inc=sFold,
                          waits=[(sAB, p.c(sAB))])
                evac_of[wb] = p.c(sFold)
        foldw = (sFold, p.c(sFold))

        self.arena_to(amark)
        self._mha64(wf["q"], wf["k"], wf["v"], wf["o"],
                    bqb, bqw, bkb, bkw, bob, bow, bvrow, bvw, x_tgt,
                    wguard=[foldw], alibi=True, rez=(rzc[:, 0:1], rzw))

    def block_l3_ffn(self):
        nc, p = self.nc, self.p
        x_tgt = self.x_target()
        self.arena_reset()

        rta = self.inp("l3_router", [8, 128, 8], F32)
        w1a = self.inp("l3_w1", [8, 8, 128, 4096], F32)
        w2a = self.inp("l3_w2", [8, 32, 128, D], F32)
        b2a = self.inp("l3_b2", [8, D], F32)
        b1c, b1w = self.load_f32("l3_b1c", [128, 8, 32])
        H3 = [nc.dram_tensor(self.fresh("l3h"), [32, 128, S], BF16).ap()
              for _ in range(2)]
        gT_hbm = nc.dram_tensor(self.fresh("l3g"), [8, S], BF16).ap()

        xh_sem, xh_cnt = self.layernorm_to_xhat(waits=self.ln_waits())
        xw = (xh_sem, xh_cnt)

        rt_sb = self.ab([128, 8, 8], BF16)
        gsc = self.ab([128, 4, 8], F32)
        gm = self.ab([128, 8], F32)
        s2 = self.ab([128, 16], F32)
        gates = self.ab([128, 32, 8], F32)
        gbf = self.ab([128, 8], BF16)
        b2sb = self.ab([8, D], BF16)
        gTs = self.ab([8, TOKCH], BF16)
        w1b = self.ab([128, 2, 8, 128], BF16)
        hbuf = self.ab([128, 2, TOKCH], BF16)
        w2h = self.ab([128, 32, TOKCH], BF16)
        hgp = self.ab([128, 2, 8, TOKCH], BF16)
        ybf = self.ab([128, 2, TOKCH], BF16)
        ytr = self.ab([128, 4, 128], F32)
        rld = self.fresh("rld")
        mmg = self.fresh("mmg")
        sGt = self.fresh("sGt")
        sGtT = self.fresh("sGtT")
        sGst = self.fresh("sGst")
        p.dma("gpsimd", rt_sb, rta.rearrange("kc p m -> p kc m"), inc=rld,
              waits=self.arena_guard)
        p.dma("gpsimd", b2sb, b2a, inc=rld)
        rldv = p.c(rld)

        for tt in range(32):
            t0 = tt * 128
            pbk = self.pb[6][:, 0:8]
            for ki in range(8):
                w = [xw, (rld, rldv), (sGt, max(0, 19 * tt - 16))] if ki == 0 else []
                p.mm(pbk, self.xhat[:, ki, t0:t0 + 128], rt_sb[:, ki, :],
                     start=(ki == 0), stop=(ki == 7), waits=w,
                     inc=mmg if ki == 7 else None)
            p.act(gsc[:, 0, :], pbk, AF.Sigmoid,
                  waits=[(mmg, p.c(mmg)), (sGt, max(0, 19 * tt - 1))],
                  inc=sGt)
            def gv(fn):
                p.vop(fn, waits=[(sGt, p.c(sGt))], inc=sGt)
            gv(lambda v: v.tensor_copy(s2[:, 0:8], gsc[:, 0, :]))
            gv(lambda v: v.tensor_copy(s2[:, 8:16], gsc[:, 0, :]))
            gv(lambda v: v.tensor_tensor(
                gsc[:, 1, :], s2[:, 1:9], gsc[:, 0, :], ALU.is_gt))
            for sh in range(2, 8):
                gv(lambda v, sh=sh: v.tensor_tensor(
                    gsc[:, 2, :], s2[:, sh:sh + 8], gsc[:, 0, :], ALU.is_gt))
                gv(lambda v: v.tensor_add(
                    gsc[:, 1, :], gsc[:, 1, :], gsc[:, 2, :]))
            gv(lambda v: v.tensor_single_scalar(
                gsc[:, 3, :], gsc[:, 1, :], 1.5, ALU.is_lt))
            gv(lambda v, tt=tt: v.tensor_mul(
                gates[:, tt, :], gsc[:, 3, :], gsc[:, 0, :]))
            gv(lambda v, tt=tt: v.tensor_copy(gbf, gates[:, tt, :]))
            gtv = p.c(sGt)
            p.emit("tensor", lambda t: t.transpose(
                self.ptps[0:8, 0:128], gbf, self.id_bf[:]),
                waits=[(sGt, gtv), self.initw, (sGtT, p.c(sGtT))], inc=sGtT)
            p.vop(lambda v: v.tensor_copy(gTs[:, 0:128], self.ptps[0:8, 0:128]),
                  waits=[(sGtT, p.c(sGtT)), (sGst, 16 * tt)], inc=sGtT)
            p.dma("sync", gT_hbm[:, t0:t0 + 128], gTs[:, 0:128], inc=sGst,
                  waits=[(sGtT, p.c(sGtT))])
        gstw = (sGst, p.c(sGst))
        gatew = (sGt, p.c(sGt))
        gtT_total = p.c(sGtT)
        if DBG_L3:
            dga = nc.dram_tensor("dbg_gT", [8, S], BF16, kind="ExternalOutput").ap()
            p.dma("gpsimd", dga, gT_hbm, inc="dbg", waits=[gstw])
            dgm = nc.dram_tensor("dbg_gates", [128, 32, 8], F32,
                                 kind="ExternalOutput").ap()
            p.dma("gpsimd", dgm, gates, inc="dbg", waits=[gatew])
        if L3FFN_PH == "gate":
            sem = self.fresh("cp")
            p.dma("sync", x_tgt[:, :], self.x_cur[:, :], inc=sem,
                  waits=[gstw, gatew])
            self.x_cur = x_tgt
            self.store_sem = (sem, p.c(sem))
            return

        gload = self.fresh("gload")
        sIni = self.fresh("sIni")
        sInSt = self.fresh("sInSt")
        xld = self.fresh("xld")
        mmg_tc = 0
        for tc in range(NT):
            t0 = tc * TOKCH
            xbuf = self.xrs[:]
            p.dma("sync", xbuf, self.fm(self.x_cur, t0, TOKCH), inc=xld,
                  waits=[(sInSt, 16 * tc)] if tc >= 1 else [])
            xldv = p.c(xld)
            p.dma("sync", gTs, gT_hbm[:, t0:t0 + TOKCH], inc=gload,
                  waits=[gstw, (mmg, mmg_tc)])
            gldv = p.c(gload)
            for mi in range(FT):
                bki = 4 + mi % 2
                pbY = self.pb[bki]
                if L3FFN_PH != "init_nomm":
                    p.mm(pbY, b2sb[:, mi * 128:(mi + 1) * 128], gTs,
                         start=True, stop=True,
                         waits=[(gload, gldv), (rld, rldv),
                                (sIni, p.c(sIni))], inc=mmg)
                else:
                    p.emit("tensor", lambda t: t.engine_nop(),
                           waits=[(gload, gldv)], inc=mmg)
                if L3FFN_PH == "init_noadd":
                    p.vop(lambda v, mi=mi, xbuf=xbuf: v.tensor_add(
                        xbuf[:, mi], xbuf[:, mi], xbuf[:, mi]),
                        waits=[(mmg, p.c(mmg)), (xld, xldv)], inc=sIni)
                else:
                    p.vop(lambda v, mi=mi, xbuf=xbuf, pbY=pbY: v.tensor_add(
                        xbuf[:, mi], xbuf[:, mi], pbY),
                        waits=[(mmg, p.c(mmg)), (xld, xldv)], inc=sIni)
                self.flush_v(sIni)
            mmg_tc = p.c(mmg)
            p.dma("sync", self.fm(x_tgt, t0, TOKCH), xbuf,
                  inc=sInSt, waits=[(sIni, p.c(sIni))])
        iniw = (sInSt, p.c(sInSt))
        ini_cnt = p.c(sIni)
        if L3FFN_PH in ("init", "init_noadd", "init_nomm"):
            self.x_cur = x_tgt
            self.store_sem = iniw
            return

        wld = self.fresh("wld")
        w2ld = self.fresh("w2ld")
        hld = self.fresh("hld")
        sMM = self.fresh("sMM")
        sGel = self.fresh("sGel")
        sHSt = self.fresh("sHSt")
        sY = self.fresh("sY")
        sEvy = self.fresh("sEvy")
        sTr = self.fresh("sTr")
        sAdd = self.fresh("sAdd")
        wld_of_buf = {0: 0, 1: 0}
        gel_of_buf = {0: 0, 1: 0}
        h_store = {0: 0, 1: 0}
        h_of_buf = {0: 0, 1: 0}
        hst_of_par = {0: 0, 1: 0}
        w2_read = 0
        ybank_free = 0
        ybf_free = {0: 0, 1: 0}
        tr_read = 0
        ytr_free = 0
        for e in range(8):
            par = e % 2
            for hc in range(32):
                wb = hc % 2
                p.dma("gpsimd", w1b[:, wb],
                      w1a[e, :, :, hc * 128:(hc + 1) * 128].rearrange(
                          "kc p m -> p kc m"),
                      inc=wld, waits=self.arena_guard + [(sMM, wld_of_buf[wb])])
                wldv = p.c(wld)
                for tc in range(NT):
                    t0 = tc * TOKCH
                    bki = 4 + tc % 2
                    pbk = self.pb[bki]
                    gb = tc % 2
                    w = [xw, (wld, wldv), (sGel, gel_of_buf[gb])]
                    if e >= 2 and hc == 0 and tc == 0 and L3FFN_PH != "w1":
                        w.append((sY, hst_of_par[par]))
                    elif e >= 2 and hc == 0 and tc == 0:
                        w.append((sHSt, hst_of_par[par]))
                    for ki in range(8):
                        p.mm(pbk, w1b[:, wb, ki, :],
                             self.xhat[:, ki, t0:t0 + TOKCH],
                             start=(ki == 0), stop=(ki == 7),
                             waits=w if ki == 0 else [],
                             inc=sMM if ki == 7 else None)
                    mmv = p.c(sMM)
                    if tc == NT - 1:
                        wld_of_buf[wb] = mmv
                    p.act(hbuf[:, gb], pbk, AF.Gelu,
                          bias=b1c[:, e, hc:hc + 1],
                          waits=[(sMM, mmv), b1w, (sHSt, h_store[gb])], inc=sGel)
                    gel_of_buf[gb] = p.c(sGel)
                    p.dma("sync", H3[par][hc, :, t0:t0 + TOKCH], hbuf[:, gb],
                          inc=sHSt, waits=[(sGel, p.c(sGel))])
                    h_store[gb] = p.c(sHSt)
            hw = (sHSt, p.c(sHSt))
            if DBG_L3 and e == 0:
                dha = nc.dram_tensor("dbg_H0", [128, S], BF16,
                                     kind="ExternalOutput").ap()
                p.dma("gpsimd", dha, H3[0][0], inc="dbg", waits=[hw])
            if L3FFN_PH == "w1":
                hst_of_par[par] = p.c(sHSt)
                continue
            for dh in range(2):
                p.dma("gpsimd", w2h,
                      w2a[e, :, :, dh * 512:(dh + 1) * 512].rearrange(
                          "kc p m -> p kc m"),
                      inc=w2ld, waits=self.arena_guard + [(sY, w2_read)])
                w2v = p.c(w2ld)
                for tg in range(NT):
                    g0 = tg * TOKCH
                    for hcg in range(4):
                        hb = hcg % 2
                        p.dma("sync", hgp[:, hb],
                              H3[par][8 * hcg:8 * hcg + 8, :,
                                      g0:g0 + TOKCH].rearrange("hc p t -> p hc t"),
                              inc=hld, waits=[hw, (sY, h_of_buf[hb])])
                        hv = p.c(hld)
                        for hc8 in range(8):
                            for st in range(4):
                                w = []
                                if hc8 == 0 and st == 0:
                                    w = [(w2ld, w2v), (hld, hv)]
                                    if hcg == 0:
                                        w.append((sEvy, ybank_free))
                                        if e == 0 and dh == 0 and tg == 0:
                                            w.append((sIni, ini_cnt))
                                p.mm(self.pb[st],
                                     hgp[:, hb, hc8, st * 128:(st + 1) * 128],
                                     w2h[:, 8 * hcg + hc8, :],
                                     start=(hcg == 0 and hc8 == 0),
                                     stop=(hcg == 3 and hc8 == 7), waits=w,
                                     inc=sY if st == 3 else None)
                        h_of_buf[hb] = p.c(sY)
                    w2_read = p.c(sY)
                    sYv = p.c(sY)
                    for st in range(4):
                        yb = st % 2
                        p.vop(lambda v, st=st, tg=tg, e=e, yb=yb:
                              v.tensor_scalar_mul(
                                  ybf[:, yb], self.pb[st],
                                  gates[:, 4 * tg + st, e:e + 1]),
                              waits=[(sY, sYv), gatew, (sTr, ybf_free[yb])],
                              inc=sEvy)
                        evv = p.c(sEvy)
                        for mi4 in range(4):
                            wtr = []
                            if mi4 == 0:
                                wtr = [(sEvy, evv), self.initw, (sTr, tr_read)]
                                if e == 0 and dh == 0 and tg == 0 and st == 0:
                                    wtr.append((sGtT, gtT_total))
                            p.emit("tensor", lambda t, mi4=mi4, yb=yb: t.transpose(
                                self.ptps[:, mi4 * 128:mi4 * 128 + 128],
                                ybf[:, yb, mi4 * 128:(mi4 + 1) * 128],
                                self.id_bf[:]),
                                waits=wtr, inc=sTr if mi4 == 3 else None)
                        trv = p.c(sTr)
                        p.vop(lambda v: v.tensor_copy(ytr, self.ptps[:, 0:512]),
                              waits=[(sTr, trv), (sAdd, ytr_free)], inc=sTr)
                        if DBG_L3 and e == 0 and dh == 0 and tg == 0 and st == 0:
                            dya = nc.dram_tensor("dbg_y0", [128, 4, 128], F32,
                                                 kind="ExternalOutput").ap()
                            p.dma("gpsimd", dya, ytr, inc="dbg",
                                  waits=[(sTr, p.c(sTr))])
                        ybf_free[yb] = p.c(sTr)
                        tr_read = p.c(sTr)
                        p.emit("gpsimd", lambda g, dh=dh, g0=g0, st=st: g.dma_start(
                            out=x_tgt[dh * 512:(dh + 1) * 512,
                                      g0 + st * 128:g0 + st * 128 + 128].rearrange(
                                "(ft p) t -> p ft t", p=128),
                            in_=ytr, accum_op=ALU.add),
                            waits=[(sTr, p.c(sTr)), iniw], inc=sAdd, amt=16)
                        ytr_free = p.c(sAdd)
                    ybank_free = p.c(sEvy)
            if L3FFN_PH != "w1":
                hst_of_par[par] = p.c(sY)
        self.x_cur = x_tgt
        self.store_sem = iniw if L3FFN_PH == "w1" else (sAdd, p.c(sAdd))

    def block_l2_ffn(self):
        nc, p = self.nc, self.p
        x_tgt = self.x_target()
        self.arena_reset()
        DH = 4096

        wexpa = self.inp("l2_wexp", [8, 128, DH], F32)
        wcona = self.inp("l2_wcon", [32, 128, D], F32)
        bupa = [self.inp(f"l2_bup{s}", [32, 128, 128], F32) for s in range(7)]
        bdna = [self.inp(f"l2_bdn{s}", [32, 128, 128], F32) for s in range(7)]
        bupd, bupdw = self.load_f32("l2_bupd", [128, 5, 4, 32])
        bdnd, bdndw = self.load_f32("l2_bdnd", [128, 5, 4, 32])
        bexpc, bexw = self.load_f32("l2_bexpc", [128, 32])
        bupc, bupw = self.load_f32("l2_bupc", [128, 32])
        bconc, bconw = self.load_f32("l2_bconc", [128, 8])
        bdbc, bdbw = self.load_f32("l2_bdbc", [128, 8])

        xh_sem, xh_cnt = self.layernorm_to_xhat(waits=self.ln_waits())
        xw = (xh_sem, xh_cnt)

        hbig = self.ab([128, 32, TOKCH], BF16)
        wst = self.ab([128, 2, 8, 128], BF16)
        bst = self.ab([128, 2, 32, 128], BF16)
        wcs = self.ab([128, 2, 32, 128], BF16)
        tmp2 = self.ab([128, 2, TOKCH], F32)
        yt = self.ab([128, 2, TOKCH], F32)
        wld = self.fresh("wld")
        bld = self.fresh("bld2")
        sMM = self.fresh("sMM")
        sEvc = self.fresh("sEvc")
        sBf = self.fresh("sBf")
        sSt = self.fresh("sSt")
        xld = self.fresh("xld")

        wld_of = {0: 0, 1: 0}
        bld_of = {0: 0, 1: 0}
        bld_of2 = {0: 0, 1: 0}
        ev_of = {i: 0 for i in range(8)}
        ev_yt = {0: 0, 1: 0}
        h_read = 0
        wcon_read = 0

        for tc in range(NT):
            t0 = tc * TOKCH
            xbuf = self.xrs[:]
            p.dma("sync", xbuf, self.fm(self.x_cur, t0, TOKCH), inc=xld,
                  waits=[(sSt, p.c(sSt))])
            xldv = p.c(xld)
            for hc in range(32):
                wb = hc % 2
                p.dma("gpsimd", wst[:, wb],
                      wexpa[:, :, hc * 128:(hc + 1) * 128].rearrange(
                          "kc p m -> p kc m"),
                      inc=wld, waits=self.arena_guard + [(sMM, wld_of[wb])])
                wldv = p.c(wld)
                bki = 4 + hc % 2
                pbk = self.pb[bki]
                for ki in range(8):
                    w = [xw, (wld, wldv), (sEvc, ev_of[bki]),
                         (sBf, h_read), (sMM, wcon_read)] if ki == 0 else []
                    p.mm(pbk, wst[:, wb, ki, :], self.xhat[:, ki, t0:t0 + TOKCH],
                         start=(ki == 0), stop=(ki == 7), waits=w,
                         inc=sMM if ki == 7 else None)
                mmv = p.c(sMM)
                wld_of[wb] = mmv
                p.act(hbig[:, hc, :], pbk, AF.Identity,
                      bias=bexpc[:, hc:hc + 1],
                      waits=[(sMM, mmv), bexw], inc=sEvc)
                ev_of[bki] = p.c(sEvc)
            hv = p.c(sEvc)
            if DBG_L2 and tc == 0:
                dh1 = nc.dram_tensor("dbg_h1", [128, 32, TOKCH], BF16,
                                     kind="ExternalOutput").ap()
                p.dma("gpsimd", dh1, hbig, inc="dbg", waits=[(sEvc, hv)])
            def bar():
                return [(sMM, p.c(sMM)), (sEvc, p.c(sEvc)), (sBf, p.c(sBf))]
            for half, (blk, diag, dw) in enumerate(
                    (((bupa, bupd, bupdw),) if L2FFN_UPONLY else
                     ((bupa, bupd, bupdw), (bdna, bdnd, bdndw)))
                    if L2FFN_BF else ()):
                for s in range(L2FFN_NBLK):
                    bb = s % 2
                    p.dma("gpsimd", bst[:, bb],
                          blk[s].rearrange("q p m -> p q m"), inc=bld,
                          waits=self.arena_guard + [(sMM, bld_of[bb])])
                    bldv = p.c(bld)
                    stage_w = bar()
                    if DBG_L2 and tc == 0 and s == 0:
                        stage_w.append(("dbg", 16 if half == 0 else 32))
                    for q in range(32):
                        bki = 4 + q % 2
                        pbk = self.pb[bki]
                        w = [(bld, bldv), (sEvc, ev_of[bki])] + stage_w
                        p.mm(pbk, bst[:, bb, q, :], hbig[:, q, :],
                             start=True, stop=True, waits=w, inc=sMM)
                        mmv = p.c(sMM)
                        if q >= 30:
                            bld_of[bb] = mmv
                        p.vop(lambda v, q=q, pbk=pbk: v.tensor_copy(
                            hbig[:, q, :], pbk),
                            waits=[(sMM, mmv), (sBf, p.c(sBf))], inc=sEvc)
                        ev_of[bki] = p.c(sEvc)
                for si, s in enumerate(range(7, 7 + L2FFN_NPAIR) if L2FFN_PAIRS else []):
                    stride = 2 ** (s - 7)
                    stage_w2 = bar()
                    for pr in range(16):
                        qa = (pr // stride) * 2 * stride + (pr % stride)
                        qb = qa + stride
                        p.vop(lambda v, qb=qb, si=si, qa=qa, dg=diag: v.tensor_scalar(
                            tmp2[:, 0], hbig[:, qb, :],
                            dg[:, si, 1, qa:qa + 1], None, ALU.mult),
                            waits=stage_w2 + [dw, (sBf, p.c(sBf))],
                            inc=sBf)
                        p.vop(lambda v, qa=qa, si=si, dg=diag: v.tensor_scalar(
                            tmp2[:, 1], hbig[:, qa, :],
                            dg[:, si, 2, qa:qa + 1], None, ALU.mult),
                            waits=[(sBf, p.c(sBf))], inc=sBf)
                        p.vop(lambda v, qa=qa, si=si, dg=diag: v.scalar_tensor_tensor(
                            hbig[:, qa, :], hbig[:, qa, :],
                            dg[:, si, 0, qa:qa + 1], tmp2[:, 0],
                            ALU.mult, ALU.add),
                            waits=[(sBf, p.c(sBf))], inc=sBf)
                        p.vop(lambda v, qb=qb, si=si, qa=qa, dg=diag: v.scalar_tensor_tensor(
                            hbig[:, qb, :], hbig[:, qb, :],
                            dg[:, si, 3, qa:qa + 1], tmp2[:, 1],
                            ALU.mult, ALU.add),
                            waits=[(sBf, p.c(sBf))], inc=sBf)
                if DBG_L2 and tc == 0:
                    nm = f"dbg_h{2 + half * 2}"
                    dh2 = nc.dram_tensor(nm, [128, 32, TOKCH], BF16,
                                         kind="ExternalOutput").ap()
                    p.dma("gpsimd", dh2, hbig, inc="dbg",
                          waits=[(sBf, p.c(sBf)), (sEvc, p.c(sEvc))])
                if half == 0 and not L2FFN_UPONLY:
                    gelu_w = bar() + [bupw]
                    for q in range(32):
                        p.act(hbig[:, q, :], hbig[:, q, :], AF.Gelu,
                              bias=bupc[:, q:q + 1],
                              waits=gelu_w if q == 0 else [], inc=sBf)
            bfv = p.c(sBf)
            for mi in range(FT):
                bki = mi % 2
                pbk = self.pb[bki]
                wb2 = mi % 2
                p.dma("gpsimd", wcs[:, wb2],
                      wcona[:, :, mi * 128:(mi + 1) * 128].rearrange(
                          "kc p m -> p kc m"), inc=bld,
                      waits=self.arena_guard + [(sMM, bld_of2[wb2])])
                bldv = p.c(bld)
                for ki in range(32):
                    w = [(bld, bldv), (sBf, bfv),
                         (sEvc, ev_of[bki])] if ki == 0 else []
                    p.mm(pbk, wcs[:, wb2, ki, :], hbig[:, ki, :],
                         start=(ki == 0), stop=(ki == 31), waits=w,
                         inc=sMM if ki == 31 else None)
                mmv = p.c(sMM)
                bld_of2[wb2] = mmv
                p.vop(lambda v, mi=mi, bki=bki: v.tensor_scalar(
                    yt[:, bki], self.pb[bki],
                    bconc[:, mi:mi + 1], None, ALU.add),
                    waits=[(sMM, mmv), bconw, (sEvc, ev_yt[bki])], inc=sEvc)
                p.vop(lambda v, mi=mi, bki=bki: v.tensor_scalar(
                    yt[:, bki], yt[:, bki], bdbc[:, mi:mi + 1], None, ALU.add),
                    waits=[(sEvc, p.c(sEvc)), bdbw], inc=sEvc)
                p.vop(lambda v, mi=mi, bki=bki, xbuf=xbuf: v.tensor_add(
                    xbuf[:, mi], xbuf[:, mi], yt[:, bki]),
                    waits=[(sEvc, p.c(sEvc)), (xld, xldv)], inc=sEvc)
                ev_of[bki] = p.c(sEvc)
                ev_yt[bki] = p.c(sEvc)
            h_read = p.c(sBf)
            wcon_read = p.c(sMM)
            p.dma("sync", self.fm(x_tgt, t0, TOKCH), xbuf,
                  inc=sSt, waits=[(sEvc, p.c(sEvc))])
        self.x_cur = x_tgt
        self.store_sem = (sSt, p.c(sSt))

    def block_l2_attn(self):
        nc, p = self.nc, self.p
        x_tgt = self.x_target()
        self.arena_reset()
        KD, VD, NH, C = 128, 256, 8, 128
        NCH = S // C

        wqa = self.inp("l2_awq", [8, 128, D], F32)
        wka = self.inp("l2_awk", [8, 128, D], F32)
        wva = self.inp("l2_awv", [8, 128, 2048], F32)
        wga = self.inp("l2_awg", [8, 128, 2048], F32)
        wab_ap = self.inp("l2_awab", [8, 128, 16], F32)
        woa = self.inp("l2_awo", [16, 128, D], F32)
        cqk, cqkw = self.load_f32("l2_acqk", [128, 2, 8, 4])
        cvv, cvvw = self.load_f32("l2_acv", [128, 16, 4])
        dtb, dtbw = self.load_f32("l2_adtb", [128, 8])
        alg_ap = self.inp("l2_aalog", [1, 8], F32)
        onr_ap = self.inp("l2_aonorm", [1, VD], F32)
        blkm_ap = self.inp("l2_ablkm", [128, 128], F32)

        xh_sem, xh_cnt = self.layernorm_to_xhat(waits=self.ln_waits())
        xw = (xh_sem, xh_cnt)

        qh = self.ab([128, S], BF16)
        kh = self.ab([128, S], BF16)
        vh = self.ab([128, 2, S], BF16)
        gh = self.ab([128, 2, S], BF16)
        wsl = self.ab([128, 2048], BF16)
        wslv = wsl.rearrange("p (a b) -> p a b", a=8)
        wos = self.ab([128, 2, D], BF16)
        wabs = self.ab([128, 8, 16], BF16)
        A_bf = self.ab([128, C], BF16)
        G_bf = self.ab([128, C], BF16)
        S0b = self.ab([128, VD], BF16)
        Rbf = self.ab([128, VD], BF16)
        Ubf = self.ab([128, VD], BF16)
        urv = self.ab([128, VD], BF16)
        vtok = self.ab([128, VD], BF16)
        idn = self.ab([128, C], BF16)
        blkm = self.ab([128, C], BF16)
        sel = self.ab([16, 8, 128], BF16)
        Ub = self.ab([16, 8, 256], BF16)
        sqd = wsl[:, 0:512].rearrange("p (a b) -> p a b", a=4)
        Astr = wsl[0:16, 512:1536].rearrange("p (a b) -> p a b", a=8)
        B16 = wsl[0:16, 1536:1552]
        rhsb = wsl[0:16, 1552:1808]
        ktok = A_bf
        obf = Rbf
        gtk = urv
        Ball = urv[:, 0:C]
        Md = vtok[:, 0:C]
        abt = self.ab([128, 32, 16], F32)
        onb = self.ab([128, VD], F32)
        nega = self.ab([128, 8], F32)
        gcol = self.ab([128, 32], F32)
        gam = self.ab([128, 32], F32)
        rev = self.ab([128, 32], F32)
        bcol = self.ab([128, 32], F32)
        csc = self.ab([128, 16], F32)
        dsb = self.ab([128, C], F32)
        S0f = self.ab([128, VD], F32)
        osb = self.ab([128, 2, VD], F32)
        yst = self.ab([128, 4, C], F32)
        cvtmp = self.xrs[:].rearrange("p a b -> p (a b)")
        algs = self.rows[0:1, 0, 0:8]
        onrs = self.rows[0:1, 1, 0:VD]

        maskM = self.ab([128, 128], F32)
        triI = self.ab([128, 128], F32)
        triR = self.ab([128, 128], F32)
        id_f32 = self.ab([128, 128], F32)
        sA = self.fresh("sA")
        sD = self.fresh("sDm")

        extraw = []

        def W():
            return [(sA, p.c(sA)), (sD, p.c(sD)), self.initw, xw,
                    cqkw, cvvw, dtbw] + list(extraw)

        def E(eng, fn):
            p.emit(eng, fn, waits=W(), inc=sA)

        def DM(out, in_, accum=False):
            if accum:
                p.emit("gpsimd", lambda g, out=out, in_=in_: g.dma_start(
                    out=out, in_=in_, accum_op=ALU.add), waits=W(), inc=sD,
                    amt=16)
            else:
                p.dma("gpsimd", out, in_, inc=sD, waits=W())

        def FL():
            self.flush_v(sA, waits=W())

        E("gpsimd", lambda g: g.memset(maskM[:], 0.0))
        E("gpsimd", lambda g: g.affine_select(
            out=maskM[:], in_=maskM[:], compare_op=ALU.is_ge, fill=-1e30,
            base=-1, pattern=[[1, 128]], channel_multiplier=-1))
        E("gpsimd", lambda g: g.memset(triI[:], 1.0))
        E("gpsimd", lambda g: g.affine_select(
            out=triI[:], in_=triI[:], compare_op=ALU.is_ge, fill=0.0,
            base=0, pattern=[[1, 128]], channel_multiplier=-1))
        E("gpsimd", lambda g: g.memset(triR[:], 1.0))
        E("gpsimd", lambda g: g.affine_select(
            out=triR[:], in_=triR[:], compare_op=ALU.is_ge, fill=0.0,
            base=-1, pattern=[[-1, 128]], channel_multiplier=1))
        E("gpsimd", lambda g: g.memset(id_f32[:], 1.0))
        E("gpsimd", lambda g: g.affine_select(
            out=id_f32[:], in_=id_f32[:], compare_op=ALU.is_equal, fill=0.0,
            base=0, pattern=[[-1, 128]], channel_multiplier=1))

        cpy = self.fresh("sXc")
        p.dma("sync", x_tgt[:, :], self.x_cur[:, :], inc=cpy,
              waits=W() + self.ln_waits())
        extraw.append((cpy, p.c(cpy)))

        for i in range(8):
            E("tensor", lambda t, i=i: t.matmul(
                self.pb[2][0:16, 0:C], self.id_bf[:, 16 * i:16 * i + 16],
                self.id_bf[:], start=True, stop=True))
            E("scalar", lambda s, i=i: s.copy(sel[:, i, :],
                                              self.pb[2][0:16, 0:C]))
        E("vector", lambda v: v.tensor_scalar_mul(idn, self.id_bf[:], -1.0))
        DM(blkm, blkm_ap)
        DM(algs, alg_ap)
        DM(onrs, onr_ap)
        DM(wabs, wab_ap.rearrange("kc p m -> p kc m"))
        E("scalar", lambda s: s.activation(algs, algs, AF.Exp))
        E("vector", lambda v: v.tensor_scalar_mul(algs, algs, -1.0))
        E("tensor", lambda t: t.matmul(self.pb[6][:, 0:8], self.ones_row[:],
                                       algs, start=True, stop=True))
        E("scalar", lambda s: s.copy(nega, self.pb[6][:, 0:8]))
        E("tensor", lambda t: t.matmul(self.pb[6][:, 0:VD], self.ones_row[:],
                                       onrs, start=True, stop=True))
        E("scalar", lambda s: s.copy(onb, self.pb[6][:, 0:VD]))
        for ch in range(NCH):
            t0 = ch * C
            for ki in range(8):
                p.emit("tensor", lambda t, ki=ki, t0=t0: t.matmul(
                    self.pb[6][:, 0:16], self.xhat[:, ki, t0:t0 + C],
                    wabs[:, ki, :], start=(ki == 0), stop=(ki == 7)),
                    waits=W(), inc=sA)
            E("scalar", lambda s, ch=ch: s.copy(abt[:, ch, :],
                                                self.pb[6][:, 0:16]))

        for h in range(min(NH, L2A_NH)):
            DM(wslv[:, :, 0:128],
               wqa[:, :, h * 128:(h + 1) * 128].rearrange("kc p m -> p kc m"))
            for tc in range(NT):
                t0 = tc * TOKCH
                for ki in range(8):
                    p.emit("tensor", lambda t, ki=ki, t0=t0: t.matmul(
                        self.pb[4], wslv[:, ki, 0:128],
                        self.xhat[:, ki, t0:t0 + TOKCH],
                        start=(ki == 0), stop=(ki == 7)), waits=W(), inc=sA)
                E("scalar", lambda s, t0=t0: s.copy(qh[:, t0:t0 + TOKCH],
                                                    self.pb[4]))
            DM(wslv[:, :, 0:128],
               wka[:, :, h * 128:(h + 1) * 128].rearrange("kc p m -> p kc m"))
            for tc in range(NT):
                t0 = tc * TOKCH
                for ki in range(8):
                    p.emit("tensor", lambda t, ki=ki, t0=t0: t.matmul(
                        self.pb[4], wslv[:, ki, 0:128],
                        self.xhat[:, ki, t0:t0 + TOKCH],
                        start=(ki == 0), stop=(ki == 7)), waits=W(), inc=sA)
                E("scalar", lambda s, t0=t0: s.copy(kh[:, t0:t0 + TOKCH],
                                                    self.pb[4]))
            for wsrc, dst in ((wva, vh), (wga, gh)):
                DM(wslv[:],
                   wsrc[:, :, h * 256:(h + 1) * 256].rearrange("kc p m -> p kc m"))
                for m2 in range(2):
                    for tc in range(NT):
                        t0 = tc * TOKCH
                        for ki in range(8):
                            p.emit("tensor", lambda t, ki=ki, t0=t0, m2=m2:
                                   t.matmul(
                                self.pb[4], wslv[:, ki, m2 * 128:(m2 + 1) * 128],
                                self.xhat[:, ki, t0:t0 + TOKCH],
                                start=(ki == 0), stop=(ki == 7)),
                                waits=W(), inc=sA)
                        E("scalar", lambda s, t0=t0, m2=m2, dst=dst: s.copy(
                            dst[:, m2, t0:t0 + TOKCH], self.pb[4]))
            DM(wos, woa[2 * h:2 * h + 2].rearrange("kc p m -> p kc m"))
            for srcs, wcol in ((qh[:], cqk[:, 0, h, :]), (kh[:], cqk[:, 1, h, :]),
                               (vh[:, 0], cvv[:, 2 * h, :]),
                               (vh[:, 1], cvv[:, 2 * h + 1, :])):
                E("vector", lambda v, srcs=srcs, wcol=wcol: v.tensor_scalar_mul(
                    cvtmp, srcs, wcol[:, 3:4]))
                for j, sh in ((2, 1), (1, 2), (0, 3)):
                    E("vector", lambda v, srcs=srcs, wcol=wcol, j=j, sh=sh:
                      v.scalar_tensor_tensor(
                        cvtmp[:, sh:S], srcs[:, 0:S - sh], wcol[:, j:j + 1],
                        cvtmp[:, sh:S], ALU.mult, ALU.add))
                E("scalar", lambda s, srcs=srcs: s.activation(
                    srcs, cvtmp, AF.Silu))
            for srcs in (qh, kh):
                for tc in range(NT):
                    t0 = tc * TOKCH
                    E("scalar", lambda s, srcs=srcs, t0=t0: s.activation(
                        cvtmp[:, 0:TOKCH], srcs[:, t0:t0 + TOKCH], AF.Square))
                    E("tensor", lambda t: t.matmul(
                        self.pb[6][0:1, :], self.ones_f32[:],
                        cvtmp[:, 0:TOKCH], start=True, stop=True))
                    E("scalar", lambda s: s.activation(
                        self.rows[:, 2, :], self.pb[6][0:1, :], AF.Sqrt))
                    E("vector", lambda v: v.tensor_scalar_max(
                        self.rows[:, 2, :], self.rows[:, 2, :], 1e-6))
                    E("vector", lambda v: v.reciprocal(
                        self.rows[:, 2, :], self.rows[:, 2, :]))
                    E("tensor", lambda t: t.matmul(
                        self.pb[6][:, :], self.ones_row[:],
                        self.rows[:, 2, :], start=True, stop=True))
                    E("scalar", lambda s: s.copy(self.bcast[:, 0],
                                                 self.pb[6][:, :]))
                    E("vector", lambda v, srcs=srcs, t0=t0: v.tensor_mul(
                        srcs[:, t0:t0 + TOKCH], srcs[:, t0:t0 + TOKCH],
                        self.bcast[:, 0]))
            E("vector", lambda v, h=h: v.tensor_scalar_add(
                gcol, abt[:, :, h], dtb[:, h:h + 1]))
            E("vector", lambda v: v.tensor_scalar_max(gam, gcol, 0.0))
            E("scalar", lambda s: s.activation(rev, gcol, AF.Abs))
            E("scalar", lambda s: s.activation(rev, rev, AF.Exp, scale=-1.0))
            E("scalar", lambda s: s.activation(rev, rev, AF.Ln,
                                               bias=self.ones_f32[:, 0:1]))
            E("vector", lambda v: v.tensor_add(gcol, gam, rev))
            E("vector", lambda v, h=h: v.tensor_scalar_mul(
                gcol, gcol, nega[:, h:h + 1]))
            E("scalar", lambda s, h=h: s.activation(
                bcol, abt[:, :, 8 + h], AF.Sigmoid))
            E("vector", lambda v: v.tensor_scalar_mul(bcol, bcol, 2.0))
            E("tensor", lambda t: t.matmul(self.pb[6][:, 0:32], triI[:], gcol,
                                           start=True, stop=True))
            E("scalar", lambda s: s.copy(gam, self.pb[6][:, 0:32]))
            E("tensor", lambda t: t.matmul(self.pb[6][:, 0:32], triR[:], gcol,
                                           start=True, stop=True))
            E("scalar", lambda s: s.copy(rev, self.pb[6][:, 0:32]))
            E("gpsimd", lambda g: g.memset(S0f[:], 0.0))
            E("gpsimd", lambda g: g.memset(S0b[:], 0.0))
            for ch in range(min(NCH, L2A_NCH)):
                t0 = ch * C
                qt = qh[:, t0:t0 + C]
                kt = kh[:, t0:t0 + C]
                E("scalar", lambda s, ch=ch: s.activation(
                    csc[:, 0:1], gam[:, ch:ch + 1], AF.Exp))
                E("scalar", lambda s, ch=ch: s.activation(
                    csc[:, 1:2], rev[:, ch:ch + 1], AF.Exp))
                E("scalar", lambda s, ch=ch: s.activation(
                    csc[:, 2:3], bcol[:, ch:ch + 1], AF.Ln))
                E("vector", lambda v: v.tensor_mul(
                    csc[:, 3:4], csc[:, 0:1], csc[:, 1:2]))
                E("vector", lambda v, ch=ch: v.tensor_scalar(
                    csc[:, 4:5], csc[:, 0:1], bcol[:, ch:ch + 1], None,
                    ALU.mult))
                E("vector", lambda v: v.tensor_scalar_mul(
                    csc[:, 5:6], csc[:, 4:5], -1.0))
                E("vector", lambda v, ch=ch: v.tensor_scalar_add(
                    csc[:, 6:7], csc[:, 2:3], gam[:, ch:ch + 1]))
                E("vector", lambda v, ch=ch: v.tensor_scalar_add(
                    csc[:, 7:8], gam[:, ch:ch + 1], 0.0))
                E("tensor", lambda t: t.matmul(
                    self.pb[6][0:1, 0:C], csc[:, 6:7], id_f32[:],
                    start=True, stop=True))
                E("scalar", lambda s: s.copy(self.rows[0:1, 0, 0:128],
                                             self.pb[6][0:1, 0:C]))
                E("tensor", lambda t: t.matmul(
                    self.pb[6][0:1, 0:C], csc[:, 7:8], id_f32[:],
                    start=True, stop=True))
                E("scalar", lambda s: s.copy(self.rows[0:1, 1, 0:128],
                                             self.pb[6][0:1, 0:C]))
                E("vector", lambda v: v.tensor_scalar_mul(
                    self.rows[0:1, 3, 0:128], self.rows[0:1, 1, 0:128], -1.0))
                E("tensor", lambda t, kt=kt: t.matmul(
                    self.pb[4][:, 0:VD], kt, S0b, start=True, stop=True))
                E("tensor", lambda t, qt=qt: t.matmul(
                    self.pb[5][:, 0:VD], qt, S0b, start=True, stop=True))
                E("tensor", lambda t, kt=kt: t.matmul(
                    self.pb[0][:, 0:C], kt, kt, start=True, stop=True))
                E("tensor", lambda t, kt=kt, qt=qt: t.matmul(
                    self.pb[1][:, 0:C], kt, qt, start=True, stop=True))
                E("tensor", lambda t: t.matmul(
                    self.pb[2][:, 0:C], self.ones_row[:], self.rows[0:1, 0, 0:128],
                    start=True, stop=False))
                E("tensor", lambda t: t.matmul(
                    self.pb[2][:, 0:C], self.rows[0:1, 3, 0:128], self.ones_row[:],
                    start=False, stop=True))
                E("tensor", lambda t: t.matmul(
                    self.pb[3][:, 0:C], self.ones_row[:], self.rows[0:1, 1, 0:128],
                    start=True, stop=False))
                E("tensor", lambda t: t.matmul(
                    self.pb[3][:, 0:C], self.rows[0:1, 3, 0:128], self.ones_row[:],
                    start=False, stop=True))
                E("scalar", lambda s: s.copy(dsb, self.pb[2][:, 0:C]))
                E("vector", lambda v: v.tensor_add(dsb, dsb, maskM[:]))
                E("scalar", lambda s: s.activation(dsb, dsb, AF.Exp))
                E("vector", lambda v: v.tensor_mul(
                    A_bf, dsb, self.pb[0][:, 0:C]))
                FL()
                E("scalar", lambda s: s.copy(dsb, self.pb[3][:, 0:C]))
                E("vector", lambda v: v.tensor_add(dsb, dsb, self.maskTg[:]))
                E("scalar", lambda s: s.activation(dsb, dsb, AF.Exp))
                E("vector", lambda v: v.tensor_mul(
                    G_bf, dsb, self.pb[1][:, 0:C]))
                FL()
                E("tensor", lambda t, t0=t0: t.transpose(
                    self.ptps[:, 0:C], vh[:, 0, t0:t0 + C], self.id_bf[:]))
                E("tensor", lambda t, t0=t0: t.transpose(
                    self.ptps[:, C:2 * C], vh[:, 1, t0:t0 + C], self.id_bf[:]))
                E("scalar", lambda s: s.copy(vtok, self.ptps[:, 0:VD]))
                E("vector", lambda v, ch=ch: v.tensor_scalar_mul(
                    osb[:, 0], vtok, bcol[:, ch:ch + 1]))
                E("vector", lambda v: v.scalar_tensor_tensor(
                    osb[:, 0], self.pb[4][:, 0:VD], csc[:, 5:6], osb[:, 0],
                    ALU.mult, ALU.add))
                FL()
                E("scalar", lambda s: s.copy(Rbf, osb[:, 0]))
                for i in range(8):
                    E("tensor", lambda t, i=i: t.matmul(
                        self.pb[2][0:16, 0:C],
                        self.id_bf[:, 16 * i:16 * i + 16], A_bf,
                        start=True, stop=True))
                    E("scalar", lambda s, i=i: s.copy(
                        Astr[:, i, :], self.pb[2][0:16, 0:C]))
                E("vector", lambda v: v.tensor_mul(A_bf, A_bf, blkm))
                E("tensor", lambda t: t.transpose(
                    self.ptps[:, 0:C], A_bf, self.id_bf[:]))
                E("scalar", lambda s: s.copy(Md, self.ptps[:, 0:C]))
                E("vector", lambda v: v.tensor_sub(Ball, self.id_bf[:], A_bf))
                E("tensor", lambda t: t.matmul(self.pb[2][:, 0:C], Md, A_bf,
                                               start=True, stop=True))
                E("scalar", lambda s: s.copy(sqd[:, 0], self.pb[2][:, 0:C]))
                E("tensor", lambda t: t.matmul(self.pb[3][:, 0:C], A_bf, Md,
                                               start=True, stop=True))
                E("scalar", lambda s: s.copy(sqd[:, 1], self.pb[3][:, 0:C]))
                E("tensor", lambda t: t.matmul(
                    self.pb[2][:, 0:C], sqd[:, 1], sqd[:, 0],
                    start=True, stop=True))
                E("scalar", lambda s: s.copy(sqd[:, 2], self.pb[2][:, 0:C]))
                E("tensor", lambda t: t.matmul(
                    self.pb[3][:, 0:C], sqd[:, 0], sqd[:, 1],
                    start=True, stop=True))
                E("scalar", lambda s: s.copy(sqd[:, 3], self.pb[3][:, 0:C]))
                E("tensor", lambda t: t.matmul(
                    self.pb[2][:, 0:C], sqd[:, 2], sqd[:, 3],
                    start=True, stop=True))
                E("scalar", lambda s: s.copy(sqd[:, 0], self.pb[2][:, 0:C]))
                for li in (1, 3, 0):
                    E("tensor", lambda t, li=li: t.matmul(
                        self.pb[6][:, 0:C], sqd[:, li], Ball,
                        start=True, stop=False))
                    E("tensor", lambda t: t.matmul(
                        self.pb[6][:, 0:C], self.id_bf[:], Ball,
                        start=False, stop=True))
                    E("scalar", lambda s: s.copy(Ball, self.pb[6][:, 0:C]))
                for j in range(8):
                    E("tensor", lambda t, j=j: t.matmul(
                        self.pb[3][0:16, 0:16],
                        self.id_bf[:, 16 * j:16 * j + 16],
                        Ball[:, 16 * j:16 * j + 16], start=True, stop=True))
                    E("scalar", lambda s: s.copy(B16,
                                                 self.pb[3][0:16, 0:16]))
                    E("tensor", lambda t, j=j: t.matmul(
                        self.pb[4][0:16, 0:VD],
                        idn[:, 16 * j:16 * j + 16], Rbf,
                        start=True, stop=(j == 0)))
                    for i in range(j):
                        E("tensor", lambda t, i=i, j=j: t.matmul(
                            self.pb[4][0:16, 0:VD],
                            Astr[:, i, 16 * j:16 * j + 16], Ub[:, i, :],
                            start=False, stop=(i == j - 1)))
                    E("vector", lambda v: v.tensor_scalar_mul(
                        rhsb, self.pb[4][0:16, 0:VD], -1.0))
                    E("tensor", lambda t: t.matmul(
                        self.pb[6][0:16, 0:VD], B16, rhsb,
                        start=True, stop=True))
                    E("scalar", lambda s, j=j: s.copy(
                        Ub[:, j, :], self.pb[6][0:16, 0:VD]))
                for i in range(8):
                    E("tensor", lambda t, i=i: t.matmul(
                        self.pb[6][:, 0:VD], sel[:, i, :], Ub[:, i, :],
                        start=(i == 0), stop=(i == 7)))
                E("vector", lambda v: v.tensor_scalar_mul(
                    urv, self.pb[6][:, 0:VD], csc[:, 1:2]))
                E("scalar", lambda s: s.copy(Ubf, self.pb[6][:, 0:VD]))
                E("tensor", lambda t: t.matmul(
                    self.pb[2][:, 0:VD], G_bf, Ubf, start=True, stop=True))
                E("vector", lambda v: v.tensor_scalar_mul(
                    osb[:, 1], self.pb[5][:, 0:VD], csc[:, 0:1]))
                E("vector", lambda v: v.tensor_add(
                    osb[:, 0], osb[:, 1], self.pb[2][:, 0:VD]))
                FL()
                E("tensor", lambda t, kt=kt: t.transpose(
                    self.ptps[:, 0:C], kt, self.id_bf[:]))
                E("scalar", lambda s: s.copy(ktok, self.ptps[:, 0:C]))
                E("tensor", lambda t: t.matmul(
                    self.pb[3][:, 0:VD], ktok, urv, start=True, stop=True))
                E("vector", lambda v: v.scalar_tensor_tensor(
                    S0f, S0f, csc[:, 3:4], self.pb[3][:, 0:VD],
                    ALU.mult, ALU.add))
                FL()
                E("scalar", lambda s: s.copy(S0b, S0f))
                E("scalar", lambda s: s.activation(osb[:, 1], osb[:, 0],
                                                   AF.Square))
                E("vector", lambda v: v.reduce_sum(csc[:, 8:9], osb[:, 1],
                                                   axis=AX.X))
                E("vector", lambda v: v.tensor_scalar_mul(
                    csc[:, 9:10], csc[:, 8:9], 1.0 / VD))
                E("scalar", lambda s: s.activation(
                    csc[:, 9:10], csc[:, 9:10], AF.Sqrt,
                    bias=self.ceps[:, 0:1]))
                E("vector", lambda v: v.reciprocal(csc[:, 9:10],
                                                   csc[:, 9:10]))
                E("vector", lambda v: v.tensor_scalar_mul(
                    osb[:, 0], osb[:, 0], csc[:, 9:10]))
                E("vector", lambda v: v.tensor_mul(osb[:, 0], osb[:, 0], onb))
                E("tensor", lambda t, t0=t0: t.transpose(
                    self.ptps[:, 0:C], gh[:, 0, t0:t0 + C], self.id_bf[:]))
                E("tensor", lambda t, t0=t0: t.transpose(
                    self.ptps[:, C:2 * C], gh[:, 1, t0:t0 + C], self.id_bf[:]))
                E("scalar", lambda s: s.copy(gtk, self.ptps[:, 0:VD]))
                E("scalar", lambda s: s.activation(osb[:, 1], gtk, AF.Silu))
                E("vector", lambda v: v.tensor_mul(obf, osb[:, 0], osb[:, 1]))
                E("tensor", lambda t: t.transpose(
                    self.ptps[:, 0:C], obf[:, 0:C], self.id_bf[:]))
                E("tensor", lambda t: t.transpose(
                    self.ptps[:, C:2 * C], obf[:, C:VD], self.id_bf[:]))
                E("scalar", lambda s: s.copy(vtok, self.ptps[:, 0:VD]))
                for half in range(2):
                    for mi4 in range(4):
                        mi = half * 4 + mi4
                        for vc in range(2):
                            E("tensor", lambda t, mi=mi, vc=vc: t.matmul(
                                self.pb[6][:, 0:C],
                                wos[:, vc, mi * 128:(mi + 1) * 128],
                                vtok[:, vc * 128:(vc + 1) * 128],
                                start=(vc == 0), stop=(vc == 1)))
                        E("scalar", lambda s, mi4=mi4: s.copy(
                            yst[:, mi4, :], self.pb[6][:, 0:C]))
                    DM(self.fm(x_tgt, t0, C)[:, 4 * half:4 * half + 4],
                       yst, accum=True)
        self.x_cur = x_tgt
        self.store_sem = (sD, p.c(sD))

    def block_l0_ffn(self):
        nc, p = self.nc, self.p
        x_tgt = self.x_target()
        self.arena_reset()

        w1a = self.inp("l0_w1", [8, 128, 8192], F32)
        w2a = self.inp("l0_w2", [64, 128, D], F32)
        ew_ap = self.inp("l0_ew", [2, 128, 32], F32)
        H_hbm = nc.dram_tensor(self.fresh("l0h"), [64, 128, S], BF16).ap()

        ew2 = self.ab([128, 2, 32], F32)
        ewcols = self.ab([128, 32], F32)
        sEwD = self.fresh("sEwD")
        sEwC = self.fresh("sEwC")
        p.dma("gpsimd", ew2, ew_ap.rearrange("o p a -> p o a"), inc=sEwD,
              waits=self.arena_guard)
        p.vop(lambda v: v.tensor_add(ewcols, ew2[:, 0], ew2[:, 1]),
              waits=[(sEwD, p.c(sEwD))], inc=sEwC)
        eww = (sEwC, p.c(sEwC))

        xh_sem, xh_cnt = self.layernorm_to_xhat(waits=self.ln_waits())
        xw = (xh_sem, xh_cnt)

        w1b = self.ab([128, 2, 8, 128], BF16)
        hbuf = self.ab([128, 2, TOKCH], BF16)
        sgA = self.ab([128, 2, TOKCH], F32)
        sgB = self.ab([128, 2, TOKCH], F32)
        usb = self.ab([128, 2, TOKCH], F32)
        wld = self.fresh("wld")
        sMM = self.fresh("sMM")
        sEvc = self.fresh("sEvc")
        sGel = self.fresh("sGel")
        sHSt = self.fresh("sHSt")

        evac_of = {0: 0, 1: 0}
        wld_of_buf = {0: 0, 1: 0}
        gel_of_buf = {0: 0, 1: 0}
        h_store = {0: 0, 1: 0}
        for hc in range(L0FFN_NH):
            wb = hc % 2
            p.dma("gpsimd", w1b[:, wb],
                  w1a[:, :, hc * 128:(hc + 1) * 128].rearrange("kc p m -> p kc m"),
                  inc=wld, waits=self.arena_guard + [(sMM, wld_of_buf[wb])])
            wldv = p.c(wld)
            for tc in range(NT):
                t0 = tc * TOKCH
                bki = 2 + tc % 2
                pbk = self.pb[bki]
                gb = tc % 2
                w = [xw, (wld, wldv), (sEvc, evac_of[gb])]
                for ki in range(8):
                    p.mm(pbk, w1b[:, wb, ki, :], self.xhat[:, ki, t0:t0 + TOKCH],
                         start=(ki == 0), stop=(ki == 7), waits=w if ki == 0 else [],
                         inc=sMM if ki == 7 else None)
                mmv = p.c(sMM)
                if tc == NT - 1:
                    wld_of_buf[wb] = mmv
                a = sgA[:, gb]
                c = sgB[:, gb]
                u = usb[:, gb]
                p.vop(lambda v, u=u, pbk=pbk: v.tensor_scalar_add(u, pbk, 0.0),
                      waits=[(sMM, mmv), (sGel, gel_of_buf[gb])], inc=sEvc)
                evac_of[gb] = p.c(sEvc)
                if L0FFN_GELU:
                    p.act(a, u, AF.Square, waits=[(sEvc, p.c(sEvc))], inc=sGel)
                    p.vop(lambda v, a=a, c=c, u=u: v.scalar_tensor_tensor(
                        c, a, 0.044715, u, ALU.mult, ALU.mult),
                        waits=[(sGel, p.c(sGel))], inc=sGel)
                    p.vop(lambda v, a=a, c=c, u=u: v.tensor_add(a, c, u),
                          inc=sGel)
                    p.act(c, a, AF.Sigmoid, scale=1.5957691216,
                          waits=[(sGel, p.c(sGel))], inc=sGel)
                    p.vop(lambda v, gb=gb, c=c, u=u: v.tensor_mul(
                        hbuf[:, gb], c, u),
                        waits=[(sGel, p.c(sGel)), (sHSt, h_store[gb])], inc=sGel)
                else:
                    p.vop(lambda v, gb=gb, u=u: v.tensor_copy(
                        hbuf[:, gb], u),
                        waits=[(sEvc, p.c(sEvc)), (sHSt, h_store[gb])], inc=sGel)
                gel_of_buf[gb] = p.c(sGel)
                if L0FFN_HST:
                    p.dma("sync", H_hbm[hc, :, t0:t0 + TOKCH], hbuf[:, gb],
                          inc=sHSt, waits=[(sGel, p.c(sGel))])
                    h_store[gb] = p.c(sHSt)
        hw = (sHSt, p.c(sHSt))

        if not L0FFN_W2:
            sem = self.fresh("cp")
            p.dma("sync", x_tgt[:, :], self.x_cur[:, :], inc=sem, waits=[hw])
            self.x_cur = x_tgt
            self.store_sem = (sem, p.c(sem))
            return
        amark = self.arena_mark()
        w2b = self.ab([128, 2, 8, TOKCH], BF16)
        hgp = self.ab([128, 2, 8, TOKCH], BF16)
        ybf = self.ab([128, 1024], BF16)
        w2ld = self.fresh("w2ld")
        hld = self.fresh("hld")
        sY = self.fresh("sY")
        sEv = self.fresh("sEv")
        sTr = self.fresh("sTr")
        sAd = self.fresh("sAd")
        sSt = self.fresh("sSt")
        xld = self.fresh("xld")
        w2_of_buf = {0: 0, 1: 0}
        h_of_buf = {0: 0, 1: 0}
        ytiles = [self.pb[0], self.pb[1], self.pb[2], self.pb[3]]
        tr_read = 0
        ybf_free = 0
        for tg in range(NT):
            g0 = tg * TOKCH
            xbuf = self.xrs[:]
            p.dma("sync", xbuf, self.fm(self.x_cur, g0, TOKCH), inc=xld,
                  waits=[(sSt, p.c(sSt))])
            xldv = p.c(xld)
            for half in range(2):
                for hcg in range(8):
                    wb = hcg % 2
                    wwaits = [(sY, w2_of_buf[wb])]
                    if tg == 0 and half == 0 and hcg < 2:
                        wwaits += self.arena_guard + [hw]
                    p.dma("gpsimd", w2b[:, wb],
                          w2a[8 * hcg:8 * hcg + 8, :,
                              half * 512:(half + 1) * 512].rearrange(
                              "kc p m -> p kc m"),
                          inc=w2ld, waits=wwaits)
                    p.dma("sync", hgp[:, wb],
                          H_hbm[8 * hcg:8 * hcg + 8, :, g0:g0 + TOKCH].rearrange(
                              "hc p t -> p hc t"),
                          inc=hld, waits=[hw, (sY, h_of_buf[wb])])
                    wv2 = p.c(w2ld)
                    hv = p.c(hld)
                    for hc8 in range(8):
                        for st in range(4):
                            w = []
                            if hc8 == 0 and st == 0:
                                w = [(w2ld, wv2), (hld, hv)]
                                if hcg == 0:
                                    w.append((sEv, ybf_free))
                            p.mm(ytiles[st],
                                 hgp[:, wb, hc8, st * 128:(st + 1) * 128],
                                 w2b[:, wb, hc8, :],
                                 start=(hcg == 0 and hc8 == 0),
                                 stop=(hcg == 7 and hc8 == 7), waits=w,
                                 inc=sY if st == 3 else None)
                    w2_of_buf[wb] = p.c(sY)
                    h_of_buf[wb] = p.c(sY)
                sYv = p.c(sY)
                for st in range(4):
                    p.vop(lambda v, st=st, half=half, tg=tg: v.tensor_scalar_mul(
                        ybf[:, half * 512:(half + 1) * 512],
                        ytiles[st], ewcols[:, 4 * tg + st:4 * tg + st + 1]),
                        waits=([(sY, sYv), eww] if st == 0 else [])
                        + [(sTr, p.c(sTr))],
                        inc=sEv)
                    evv = p.c(sEv)
                    for mi4 in range(4):
                        mi = half * 4 + mi4
                        p.emit("tensor", lambda t, mi4=mi4, half=half: t.transpose(
                            self.ptps[:, mi4 * 128 + half * 512:
                                      mi4 * 128 + half * 512 + 128],
                            ybf[:, half * 512 + mi4 * 128:half * 512 + mi4 * 128 + 128],
                            self.id_bf[:]),
                            waits=[(sEv, evv), self.initw,
                                   (sAd, tr_read)] if mi4 == 0 else [],
                            inc=sTr if mi4 == 3 else None)
                    trv = p.c(sTr)
                    for mi4 in range(4):
                        mi = half * 4 + mi4
                        p.vop(lambda v, mi=mi, mi4=mi4, st=st, half=half,
                              xbuf=xbuf: v.tensor_add(
                                  xbuf[:, mi, st * 128:(st + 1) * 128],
                                  xbuf[:, mi, st * 128:(st + 1) * 128],
                                  self.ptps[:, mi4 * 128 + half * 512:
                                            mi4 * 128 + half * 512 + 128]),
                              waits=[(sTr, trv), (xld, xldv)] if mi4 == 0 else [],
                              inc=sAd if mi4 == 3 else None)
                    self.flush_v(sAd)
                    tr_read = p.c(sAd)
                ybf_free = p.c(sEv)
            p.dma("sync", self.fm(x_tgt, g0, TOKCH), xbuf,
                  inc=sSt, waits=[(sAd, p.c(sAd))])
        self.x_cur = x_tgt
        self.store_sem = (sSt, p.c(sSt))

    def block_l1_attn(self):
        nc, p = self.nc, self.p
        HD = 128
        SC = float(1.0 / np.sqrt(HD))
        x_tgt = self.x_target()
        self.arena_reset()

        wq, wqw = self.load_w("l1_wq", 8, HD)
        wk, wkw = self.load_w("l1_wk", 8, HD)
        wv, wvw = self.load_w("l1_wv", 8, HD)
        wo, wow = self.load_w("l1_wo", 1, D)
        bsb, bw1 = self.load_f32("l1_bqk", [128, 2])
        bob, bw2 = self.load_f32("l1_bo", [128, 8])
        bvr_ap = self.inp("l1_bv_row", [1, HD], F32)
        bvrow = self.ab([1, HD], BF16)
        bvs = self.fresh("bld")
        p.dma("gpsimd", bvrow[:], bvr_ap, inc=bvs)
        bvw = (bvs, p.c(bvs))

        xh_sem, xh_cnt = self.layernorm_to_xhat(waits=self.ln_waits())
        xw = (xh_sem, xh_cnt)

        qT = self.ab([128, S], BF16)
        kT = self.ab([128, S], BF16)
        vtok = self.ab([128, 32, 128], BF16)
        pj = self.fresh("pj")
        mmq = self.fresh("mmq")

        evac_of_bank = {0: 0, 1: 0, 2: 0, 3: 0}
        for idx, (wsb, wwt, dstT, bcol, scale) in enumerate(
                ((wq, wqw, qT, bsb[:, 0:1], SC), (wk, wkw, kT, bsb[:, 1:2], 1.0))):
            for tc in range(NT):
                t0 = tc * TOKCH
                bki = (idx * NT + tc) % 2
                pbk = self.pb[bki][:, 0:TOKCH]
                for ki in range(8):
                    w = [xw, wwt, (pj, evac_of_bank[bki])] if ki == 0 else []
                    p.mm(pbk, wsb[:, ki, :], self.xhat[:, ki, t0:t0 + TOKCH],
                         start=(ki == 0), stop=(ki == 7), waits=w,
                         inc=mmq if ki == 7 else None)
                p.act(dstT[:, t0:t0 + TOKCH], pbk, AF.Identity, bias=bcol, scale=scale,
                      waits=[(mmq, p.c(mmq)), bw1], inc=pj)
                evac_of_bank[bki] = p.c(pj)
        for tt in range(32):
            t0 = tt * 128
            bki = 2 + tt % 2
            pbk = self.pb[bki][:, 0:128]
            for ki in range(8):
                w = [xw, wvw, (pj, evac_of_bank[bki])] if ki == 0 else []
                p.mm(pbk, self.xhat[:, ki, t0:t0 + 128], wv[:, ki, :],
                     start=(ki == 0), stop=False, waits=w)
            p.mm(pbk, self.ones_row_bf[:], bvrow[:], start=False, stop=True,
                 waits=[self.initw, bvw], inc=mmq)
            p.act(vtok[:, tt, :], pbk, AF.Identity, waits=[(mmq, p.c(mmq))], inc=pj)
            evac_of_bank[bki] = p.c(pj)
        pjv = p.c(pj)

        maskT = self.maskTg
        mko = self.initw

        ptile = self.ab([128, 2, TOKCH], BF16)
        dtmp = self.ab([128, 128], F32)
        attn_bf = self.ab([128, TOKCH], BF16)
        yt = self.ab([128, 2, TOKCH], F32)
        sS = self.fresh("sS")
        sP = self.fresh("sP")
        sPV = self.fresh("sPV")
        sNm = self.fresh("sNm")
        sOut = self.fresh("sOut")
        sEv = self.fresh("sEv")
        sSt = self.fresh("sSt")
        xld = self.fresh("xld")

        pv_read = {0: 0, 1: 0}
        s_read = {4: 0, 5: 0}
        dtmp_free = 0
        pv_free = 0
        attn_read = 0
        ev_of_yt = {0: 0, 1: 0}

        for qc in range(NT):
            q0 = qc * TOKCH
            nkt = 4 * qc + 4
            xbuf = self.xrs[:]
            p.dma("sync", xbuf, self.fm(self.x_cur, q0, TOKCH), inc=xld,
                  waits=[(sSt, p.c(sSt))])
            xldv = p.c(xld)
            for kt in range(nkt):
                k0 = kt * 128
                d0 = max(0, k0 - q0)
                bki = 4 + kt % 2
                pbS = self.pb[bki][:, 0:TOKCH]
                pv = ptile[:, kt % 2]
                p.mm(pbS[:, d0:TOKCH], kT[:, k0:k0 + 128],
                     qT[:, q0 + d0:q0 + TOKCH], start=True, stop=True,
                     waits=[(pj, pjv), (sP, s_read[bki])], inc=sS)
                sSv = p.c(sS)
                if k0 >= q0:
                    p.vop(lambda v, pbS=pbS, d0=d0: v.tensor_add(
                        dtmp[:], pbS[:, d0:d0 + 128], maskT[:]),
                        waits=[(sS, sSv), mko, (sP, dtmp_free),
                               (sPV, pv_read[kt % 2])], inc=sS)
                    p.act(pv[:, d0:d0 + 128], dtmp[:], AF.Exp,
                          waits=[(sS, p.c(sS))], inc=sP)
                    dtmp_free = p.c(sP)
                    if TOKCH - d0 > 128:
                        p.act(pv[:, d0 + 128:TOKCH], pbS[:, d0 + 128:TOKCH],
                              AF.Exp, inc=sP)
                else:
                    p.act(pv[:, d0:TOKCH], pbS[:, d0:TOKCH], AF.Exp,
                          waits=[(sS, sSv), (sPV, pv_read[kt % 2])], inc=sP)
                s_read[bki] = p.c(sP)
                sPv = p.c(sP)
                w_pv = [(sP, sPv)]
                if kt == 0:
                    w_pv.append((sNm, pv_free))
                p.mm(self.pb[2][:, d0:TOKCH], vtok[:, kt, :], pv[:, d0:TOKCH],
                     start=(kt == 0), stop=(kt == nkt - 1), waits=w_pv)
                p.mm(self.pb[3][0:1, d0:TOKCH], self.ones_bf[:], pv[:, d0:TOKCH],
                     start=(kt == 0), stop=(kt == nkt - 1), inc=sPV)
                pv_read[kt % 2] = p.c(sPV)
            sPVv = p.c(sPV)
            p.vop(lambda v: v.reciprocal(self.rows[:, 3, :], self.pb[3][0:1, 0:TOKCH]),
                  waits=[(sPV, sPVv), (sNm, p.c(sNm))], inc=sNm)
            p.mm(self.pb[6][:, 0:TOKCH], self.ones_row[:], self.rows[:, 3, :], start=True,
                 stop=True, waits=[(sNm, p.c(sNm))], inc=sNm)
            p.emit("scalar", lambda s: s.copy(self.bcast[:, 0], self.pb[6][:, 0:TOKCH]),
                   waits=[(sNm, p.c(sNm))], inc=sNm)
            p.vop(lambda v: v.tensor_mul(attn_bf[:], self.pb[2][:, 0:TOKCH],
                  self.bcast[:, 0]),
                  waits=[(sNm, p.c(sNm)), (sOut, attn_read)], inc=sNm)
            self.flush_v(sNm)
            pv_free = p.c(sNm)
            sNmv = p.c(sNm)
            for mi in range(FT):
                bki = mi % 2
                pbY = self.pb[bki][:, 0:TOKCH]
                p.mm(pbY, wo[:, 0, mi * 128:(mi + 1) * 128], attn_bf[:],
                     start=True, stop=True,
                     waits=[(sNm, sNmv), wow, (sEv, ev_of_yt[bki])], inc=sOut)
                p.act(yt[:, bki], pbY, AF.Identity, bias=bob[:, mi:mi + 1],
                      waits=[(sOut, p.c(sOut)), (xld, xldv), bw2], inc=sEv)
                p.vop(lambda v, mi=mi, bki=bki, xbuf=xbuf: v.tensor_add(
                    xbuf[:, mi], xbuf[:, mi], yt[:, bki]),
                    waits=[(sEv, p.c(sEv))], inc=sEv)
                ev_of_yt[bki] = p.c(sEv)
            attn_read = p.c(sOut)
            p.dma("sync", self.fm(x_tgt, q0, TOKCH), xbuf,
                  inc=sSt, waits=[(sEv, p.c(sEv))])
        self.x_cur = x_tgt
        self.store_sem = (sSt, p.c(sSt))

    def block_l1_ffn(self):
        nc, p = self.nc, self.p
        x_tgt = self.x_target()
        self.arena_reset()

        u1, u1w = self.load_w("l1_u1", 8, 256)
        v1, v1w = self.load_w("l1_v1", 2, 4096)
        u2, u2w = self.load_w("l1_u2", 32, 256)
        v2, v2w = self.load_w("l1_v2", 2, D)
        b1b, b1w = self.load_f32("l1_b1", [128, 32])
        b2b, b2w = self.load_f32("l1_b2", [128, 8])

        xh_sem, xh_cnt = self.layernorm_to_xhat(waits=self.ln_waits())
        xw = (xh_sem, xh_cnt)

        pT = self.ab([128, 2, TOKCH], BF16)
        hT = self.ab([128, 2, TOKCH], BF16)
        qTf = self.ab([128, 2, TOKCH], BF16)
        gsq = self.ab([128, 2, TOKCH], F32)
        yt = self.ab([128, 2, TOKCH], F32)
        sMM = self.fresh("sMM")
        sEvc = self.fresh("sEvc")
        sGel = self.fresh("sGel")
        sU2 = self.fresh("sU2")
        sSt = self.fresh("sSt")
        xld = self.fresh("xld")

        evac_of_bank = {i: 0 for i in range(8)}
        gel_of_buf = {0: 0, 1: 0}
        ev_of_yt = {0: 0, 1: 0}
        pT_read = 0
        hT_read = {0: 0, 1: 0}
        qT_read = 0

        for tc in range(NT):
            t0 = tc * TOKCH
            xbuf = self.xrs[:]
            p.dma("sync", xbuf, self.fm(self.x_cur, t0, TOKCH), inc=xld,
                  waits=[(sSt, p.c(sSt))])
            xldv = p.c(xld)
            for mi in range(2):
                pbk = self.pb[mi][:, 0:TOKCH]
                for ki in range(8):
                    w = [xw, u1w, (sEvc, evac_of_bank[mi]),
                         (sMM, pT_read)] if ki == 0 else []
                    p.mm(pbk, u1[:, ki, mi * 128:(mi + 1) * 128],
                         self.xhat[:, ki, t0:t0 + TOKCH],
                         start=(ki == 0), stop=(ki == 7), waits=w,
                         inc=sMM if ki == 7 else None)
                p.act(pT[:, mi], pbk, AF.Identity, waits=[(sMM, p.c(sMM))], inc=sEvc)
                evac_of_bank[mi] = p.c(sEvc)
            pTv = p.c(sEvc)
            for ki32 in range(32):
                bki = 2 + ki32 % 2
                pbk = self.pb[bki][:, 0:TOKCH]
                for ki in range(2):
                    w = [(sEvc, pTv), v1w,
                         (sGel, gel_of_buf[ki32 % 2]),
                         (sMM, pT_read if ki32 < 30 else 0)] if ki == 0 else []
                    p.mm(pbk, v1[:, ki, ki32 * 128:(ki32 + 1) * 128], pT[:, ki],
                         start=(ki == 0), stop=(ki == 1), waits=w,
                         inc=sMM if ki == 1 else None)
                mmv = p.c(sMM)
                if ki32 == 31:
                    pT_read = mmv
                g = gsq[:, ki32 % 2]
                p.act(g, pbk, AF.Identity, bias=b1b[:, ki32:ki32 + 1],
                      waits=[(sMM, mmv), b1w], inc=sEvc)
                evv = p.c(sEvc)
                evac_of_bank[bki] = evv
                gb = hT[:, ki32 % 2]
                p.act(self.lnsq[:, 4 + ki32 % 2], g, AF.Square,
                      waits=[(sEvc, evv), (sGel, gel_of_buf[ki32 % 2])], inc=sGel)
                p.vop(lambda v, k=ki32: v.tensor_scalar(
                    self.lnsq[:, 6], self.lnsq[:, 4 + k % 2], 0.044715, 1.0,
                    ALU.mult, ALU.add), waits=[(sGel, p.c(sGel))], inc=sGel)
                p.vop(lambda v, g=g: v.tensor_mul(
                    self.lnsq[:, 7], self.lnsq[:, 6], g), inc=sGel)
                p.act(self.lnsq[:, 6], self.lnsq[:, 7], AF.Sigmoid,
                      scale=1.5957691216, waits=[(sGel, p.c(sGel))], inc=sGel)
                p.vop(lambda v, gb=gb, g=g: v.tensor_mul(
                    gb, self.lnsq[:, 6], g),
                    waits=[(sGel, p.c(sGel)), (sU2, hT_read[ki32 % 2])], inc=sGel)
                gel_of_buf[ki32 % 2] = p.c(sGel)
                for mi in range(2):
                    w = [(sGel, gel_of_buf[ki32 % 2])]
                    if ki32 == 0:
                        w += [u2w, (sEvc, evac_of_bank[4 + mi]), (sMM, qT_read)]
                    p.mm(self.pb[4 + mi][:, 0:TOKCH],
                         u2[:, ki32, mi * 128:(mi + 1) * 128], hT[:, ki32 % 2],
                         start=(ki32 == 0), stop=(ki32 == 31), waits=w,
                         inc=sU2 if mi == 1 else None)
                hT_read[ki32 % 2] = p.c(sU2)
            for mi in range(2):
                p.act(qTf[:, mi], self.pb[4 + mi][:, 0:TOKCH], AF.Identity,
                      waits=[(sU2, p.c(sU2))], inc=sEvc)
                evac_of_bank[4 + mi] = p.c(sEvc)
            qTv = p.c(sEvc)
            for mi in range(FT):
                bki = 6 if mi % 2 == 0 else 5
                pbk = self.pb[bki][:, 0:TOKCH]
                for ki in range(2):
                    w = [(sEvc, qTv), v2w,
                         (sEvc, evac_of_bank[bki])] if ki == 0 else []
                    p.mm(pbk, v2[:, ki, mi * 128:(mi + 1) * 128], qTf[:, ki],
                         start=(ki == 0), stop=(ki == 1), waits=w,
                         inc=sMM if ki == 1 else None)
                if mi == FT - 1:
                    qT_read = p.c(sMM)
                p.act(yt[:, mi % 2], pbk, AF.Identity, bias=b2b[:, mi:mi + 1],
                      waits=[(sMM, p.c(sMM)), (xld, xldv), b2w,
                             (sEvc, ev_of_yt[mi % 2])], inc=sEvc)
                evac_of_bank[bki] = p.c(sEvc)
                p.vop(lambda v, mi=mi, xbuf=xbuf: v.tensor_add(
                    xbuf[:, mi], xbuf[:, mi], yt[:, mi % 2]),
                    waits=[(sEvc, p.c(sEvc))], inc=sEvc)
                ev_of_yt[mi % 2] = p.c(sEvc)
            p.dma("sync", self.fm(x_tgt, t0, TOKCH), xbuf,
                  inc=sSt, waits=[(sEvc, p.c(sEvc))])
        self.x_cur = x_tgt
        self.store_sem = (sSt, p.c(sSt))

    def epilogue_copy_out(self):
        p = self.p
        sem = self.fresh("cp")
        w = [self.store_sem] if self.store_sem else []
        p.dma("sync", self.x_out[:, :], self.x_cur[:, :], inc=sem, waits=w)
        self.store_sem = (sem, p.c(sem))

    def finish(self):
        p = self.p
        p.emit("gpsimd", lambda g: g.engine_nop(), waits=[self.store_sem])
        p.finalize()



def prep_l1(inputs):
    return {
        "l1_wq": _slabs(inputs["L1_attn_Wq"].T),
        "l1_wk": _slabs(inputs["L1_attn_Wk"].T),
        "l1_wv": _slabs(inputs["L1_attn_Wv"].T),
        "l1_wo": _slabs(inputs["L1_attn_Wo"].T),
        "l1_bqk": np.stack([np.asarray(inputs["L1_attn_bq"], np.float32),
                            np.asarray(inputs["L1_attn_bk"], np.float32)], axis=1),
        "l1_bo": _bias_cols(inputs["L1_attn_bo"]),
        "l1_bv_row": np.asarray(inputs["L1_attn_bv"], np.float32)[None, :],
        "l1_u1": _slabs(inputs["L1_ffn_U1"]),
        "l1_v1": _slabs(inputs["L1_ffn_V1"]),
        "l1_u2": _slabs(inputs["L1_ffn_U2"]),
        "l1_v2": _slabs(inputs["L1_ffn_V2"]),
        "l1_b1": _bias_cols(inputs["L1_ffn_b1"]),
        "l1_b2": _bias_cols(inputs["L1_ffn_b2"]),
    }


def prep_l0(inputs):
    Wqkv = np.asarray(inputs["L0_attn_Wqkv"], np.float32)
    bqkv = np.asarray(inputs["L0_attn_bqkv"], np.float32)
    ew = np.asarray(inputs["L0_ffn_expert_weights"], np.float32)
    return {
        "l0_wq": _slabs(Wqkv[:, 0:1024]),
        "l0_wk": _slabs(Wqkv[:, 1024:2048]),
        "l0_wv": _slabs(Wqkv[:, 2048:3072]),
        "l0_wout": np.ascontiguousarray(
            np.asarray(inputs["L0_attn_Wout"], np.float32).reshape(16, 64, D)),
        "l0_bq": _bias_cols(bqkv[0:1024]),
        "l0_bk": _bias_cols(bqkv[1024:2048]),
        "l0_bv_row": bqkv[2048:3072][None, :].copy(),
        "l0_bout": _bias_cols(inputs["L0_attn_bout"]),
        "l0_w1": _slabs(inputs["L0_ffn_w1"]),
        "l0_w2": _slabs(inputs["L0_ffn_w2"]),
        "l0_ew": np.ascontiguousarray(
            ew.T.reshape(2, 32, 128).transpose(0, 2, 1)),
    }


def prep_l3a(inputs):
    d = {}
    for pname, W, A, Bm, b in (
            ("q", "L3_attn_Wq", "L3_attn_Aq", "L3_attn_Bq", "L3_attn_bq"),
            ("k", "L3_attn_Wk", "L3_attn_Ak", "L3_attn_Bk", "L3_attn_bk"),
            ("v", "L3_attn_Wv", "L3_attn_Av", "L3_attn_Bv", "L3_attn_bv"),
            ("o", "L3_attn_Wo", "L3_attn_Ao", "L3_attn_Bo", "L3_attn_bo")):
        d[f"l3_w{pname}"] = _slabs(np.asarray(inputs[W], np.float32).T)
        d[f"l3_a{pname}"] = np.ascontiguousarray(np.asarray(inputs[A], np.float32))
        d[f"l3_bt{pname}"] = np.ascontiguousarray(np.asarray(inputs[Bm], np.float32).T)
    d["l3_bq"] = _bias_cols(inputs["L3_attn_bq"])
    d["l3_bk"] = _bias_cols(inputs["L3_attn_bk"])
    d["l3_bo"] = _bias_cols(inputs["L3_attn_bo"])
    d["l3_bv_row"] = np.asarray(inputs["L3_attn_bv"], np.float32)[None, :].copy()
    d["l3_rz"] = np.ascontiguousarray(np.broadcast_to(
        np.asarray(inputs["L3_attn_rezero"], np.float32).reshape(1, 1), (128, 1)))
    d["l3_router"] = _slabs(np.asarray(inputs["L3_ffn_router_w"], np.float32))
    w1 = np.asarray(inputs["L3_ffn_w1"], np.float32)
    d["l3_w1"] = np.ascontiguousarray(w1.reshape(8, 8, 128, 4096))
    w2 = np.asarray(inputs["L3_ffn_w2"], np.float32)
    d["l3_w2"] = np.ascontiguousarray(w2.reshape(8, 32, 128, D))
    b1 = np.asarray(inputs["L3_ffn_b1"], np.float32)
    d["l3_b1c"] = np.ascontiguousarray(
        b1.reshape(8, 32, 128).transpose(2, 0, 1))
    d["l3_b2"] = np.ascontiguousarray(np.asarray(inputs["L3_ffn_b2"], np.float32))
    return d


def _butterfly_host(weights):
    blocks = []
    DH = 4096
    for s in range(7):
        sg = 2 ** s
        f0 = (np.arange(DH // 2) // sg) * 2 * sg + (np.arange(DH // 2) % sg)
        f1 = f0 + sg
        w = np.asarray(weights[s], np.float32).reshape(DH // (2 * sg), sg, 2, 2)
        k = f0 // (2 * sg); t = f0 % sg
        M = np.zeros((DH, DH), np.float32)
        M[f0, f0] = w[k, t, 0, 0]; M[f0, f1] = w[k, t, 0, 1]
        M[f1, f0] = w[k, t, 1, 0]; M[f1, f1] = w[k, t, 1, 1]
        blk = np.zeros((32, 128, 128), np.float32)
        for q in range(32):
            blk[q] = M[q * 128:(q + 1) * 128, q * 128:(q + 1) * 128].T
        blocks.append(np.ascontiguousarray(blk))
    diag = np.zeros((128, 5, 4, 32), np.float32)
    for si in range(5):
        s = 7 + si
        sg = 2 ** s
        w = np.asarray(weights[s], np.float32).reshape(DH // (2 * sg), sg, 2, 2)
        st = 2 ** si
        for pr in range(16):
            qa = (pr // st) * 2 * st + (pr % st)
            p_ = np.arange(128)
            ff0 = qa * 128 + p_
            k = ff0 // (2 * sg); t = ff0 % sg
            diag[:, si, 0, qa] = w[k, t, 0, 0]
            diag[:, si, 1, qa] = w[k, t, 0, 1]
            diag[:, si, 2, qa] = w[k, t, 1, 0]
            diag[:, si, 3, qa] = w[k, t, 1, 1]
    return blocks, diag


def prep_l2a(inputs):
    d = {}
    d["l2_awq"] = _slabs(inputs["L2_attn_Wq"])
    d["l2_awk"] = _slabs(inputs["L2_attn_Wk"])
    d["l2_awv"] = _slabs(inputs["L2_attn_Wv"])
    d["l2_awg"] = _slabs(inputs["L2_attn_Wg"])
    wa = np.asarray(inputs["L2_attn_Wa"], np.float32)
    wb = np.asarray(inputs["L2_attn_Wb"], np.float32)
    d["l2_awab"] = _slabs(np.concatenate([wa, wb], axis=1))
    d["l2_awo"] = _slabs(inputs["L2_attn_Wo"])
    cq = np.asarray(inputs["L2_attn_conv_q"], np.float32).reshape(8, 128, 4)
    ck = np.asarray(inputs["L2_attn_conv_k"], np.float32).reshape(8, 128, 4)
    d["l2_acqk"] = np.ascontiguousarray(
        np.stack([cq, ck], axis=0).transpose(2, 0, 1, 3))
    d["l2_acv"] = np.ascontiguousarray(
        np.asarray(inputs["L2_attn_conv_v"], np.float32)
        .reshape(16, 128, 4).transpose(1, 0, 2))
    d["l2_adtb"] = np.ascontiguousarray(np.broadcast_to(
        np.asarray(inputs["L2_attn_dt_bias"], np.float32)[None, :], (128, 8)))
    d["l2_aalog"] = np.asarray(inputs["L2_attn_A_log"], np.float32)[None, :].copy()
    d["l2_aonorm"] = np.asarray(
        inputs["L2_attn_o_norm_w"], np.float32)[None, :].copy()
    d["l2_ablkm"] = np.kron(np.eye(8, dtype=np.float32),
                            np.ones((16, 16), np.float32))
    return d


def prep_l2f(inputs):
    d = {}
    d["l2_wexp"] = _slabs(inputs["L2_ffn_W_exp"])
    d["l2_wcon"] = _slabs(inputs["L2_ffn_W_con"])
    bup, dup = _butterfly_host(inputs["L2_ffn_up_weights"])
    bdn, ddn = _butterfly_host(inputs["L2_ffn_down_weights"])
    for s in range(7):
        d[f"l2_bup{s}"] = bup[s]
        d[f"l2_bdn{s}"] = bdn[s]
    d["l2_bupd"] = dup
    d["l2_bdnd"] = ddn
    d["l2_bexpc"] = _bias_cols(inputs["L2_ffn_b_exp"])
    d["l2_bupc"] = _bias_cols(inputs["L2_ffn_up_bias"])
    d["l2_bconc"] = _bias_cols(inputs["L2_ffn_b_con"])
    d["l2_bdbc"] = _bias_cols(inputs["L2_ffn_down_bias"])
    return d


BLOCKS = ["l0_attn", "l0_ffn", "l1_attn", "l1_ffn",
          "l2_attn", "l2_ffn", "l3_attn", "l3_ffn"]

_BLOCK_EMIT = {
    "l0_attn": lambda b: b.block_l0_attn(),
    "l0_ffn": lambda b: b.block_l0_ffn(),
    "l1_attn": lambda b: b.block_l1_attn(),
    "l1_ffn": lambda b: b.block_l1_ffn(),
    "l3_attn": lambda b: b.block_l3_attn(),
    "l3_ffn": lambda b: b.block_l3_ffn(),
    "l2_ffn": lambda b: b.block_l2_ffn(),
    "l2_attn": lambda b: b.block_l2_attn(),
}

def prep_l2(inputs):
    d = {}
    if "L2_ffn_W_exp" in inputs:
        d.update(prep_l2f(inputs))
    if "L2_attn_Wq" in inputs:
        d.update(prep_l2a(inputs))
    return d


_BLOCK_PREP = {"l0": prep_l0, "l1": prep_l1, "l3": prep_l3a, "l2": prep_l2}


def build_program(blocks):
    nc = bass.Bass("TRN2", target_bir_lowering=False, debug=False,
                   detect_race_conditions=False)
    b = Builder(nc)
    for name in blocks:
        _BLOCK_EMIT[name](b)
    b.epilogue_copy_out()
    b.finish()
    return nc, b


def kernel(**inputs):
    blocks = BLOCKS
    prepped = {}
    for fam in sorted({n.split("_")[0] for n in blocks}):
        prepped.update(_BLOCK_PREP[fam](inputs))
    x = np.asarray(inputs["L0_attn_x"], np.float32)
    nc, b = build_program(blocks)
    in_maps = []
    for c in range(8):
        batch = c % 4
        m = {"x_res": np.ascontiguousarray(x[batch].T)}
        m.update({k: v for k, v in prepped.items() if k in b.input_names})
        if batch != 0 and "l0_ew" in m:
            m = dict(m)
            m["l0_ew"] = np.zeros_like(m["l0_ew"])
        in_maps.append(m)
    res = run_bass_kernel_spmd(nc, in_maps, core_ids=list(range(8)))
    out = np.empty((B, S, D), np.float32)
    for c in range(8):
        batch, half = c % 4, c // 4
        xo = np.asarray(res.results[c]["x_out"], np.float32)
        out[batch, half * 2048:(half + 1) * 2048, :] = xo.T[half * 2048:(half + 1) * 2048]
    return out
